# version 20
# baseline (speedup 1.0000x reference)
import numpy as np
from contextlib import ExitStack

import concourse.bass as bass
import concourse.mybir as mybir
from concourse.bass_utils import run_bass_kernel_spmd

F32 = mybir.dt.float32
BF16 = mybir.dt.bfloat16
AX = mybir.AluOpType
AF = mybir.ActivationFunctionType
LN2 = float(np.log(2.0))

D = 1024
NB = 8
WBUF = 4


def qt_widths(nq):
    ws = []
    rem = nq
    while rem > 512:
        ws.append(512)
        rem -= 512
    ws.append(rem)
    return [ws[-1]] + ws[:-1]


class Ctr:
    __slots__ = ("sem", "v")

    def __init__(self, sem):
        self.sem = sem
        self.v = 0

    def inc(self, instr, n=1):
        instr.then_inc(self.sem, n)
        self.v += n
        return self.v


def build(S, TOKQ):
    NQ = TOKQ + 1
    KS = S // 128
    NCH = S // 512
    NT = S // 128
    WL = qt_widths(NQ)
    QOFF = [sum(WL[:i]) for i in range(len(WL))]
    NQP = NQ + (-NQ) % 4
    CW = [512] * (TOKQ // 512) + ([TOKQ % 512] if TOKQ % 512 else [])
    COFF = [1 + 512 * i for i in range(len(CW))]
    QHALVES = list(range((TOKQ + 511) // 512))

    nc = bass.Bass(target_bir_lowering=False, trn_type="TRN2")

    xkv_d = nc.dram_tensor("xkv", [S, D], F32, kind="ExternalInput").ap()
    xb_d = nc.dram_tensor("xb", [1, D], F32, kind="ExternalInput").ap()
    w_in = {w: nc.dram_tensor(w, [D, D], F32, kind="ExternalInput").ap()
            for w in ("wq", "wk", "wv", "wo", "wr", "wkf", "wvf")}
    b_in = {b: nc.dram_tensor(b, [1, D], F32, kind="ExternalInput").ap()
            for b in ("bq", "bk", "bv", "bo")}
    gam_d = nc.dram_tensor("gam", [128, NB], F32, kind="ExternalInput").ap()
    bet_d = nc.dram_tensor("bet", [128, NB], F32, kind="ExternalInput").ap()
    mur_d = nc.dram_tensor("mur", [128, NB], F32, kind="ExternalInput").ap()
    muk_d = nc.dram_tensor("muk", [128, NB], F32, kind="ExternalInput").ap()
    mu_d = nc.dram_tensor("mu", [128, 1], F32, kind="ExternalInput").ap()
    msk_d = nc.dram_tensor("msk", [128, 1], F32, kind="ExternalInput").ap()
    out_d = nc.dram_tensor("out", [TOKQ, D], F32, kind="ExternalOutput").ap()
    ktd = nc.dram_tensor("ktd", [D, S], BF16, kind="Internal").ap()
    vd = nc.dram_tensor("vd", [NB, NCH, 128, 528], BF16, kind="Internal").ap()
    wfd = {w: nc.dram_tensor(w + "s", [NB, 128, D], BF16, kind="Internal").ap()
           for w in ("wr", "wkf", "wvf", "wo")}

    with ExitStack() as big:
        E = big.enter_context
        qt_sb = E(nc.sbuf_tensor("qt_sb", [128, NB, NQP], BF16))
        y_sb = E(nc.sbuf_tensor("y_sb", [128, NB, NQ], F32))
        gam_sb = E(nc.sbuf_tensor("gam_sb", [128, NB], F32))
        bet_sb = E(nc.sbuf_tensor("bet_sb", [128, NB], F32))
        mur_sb = E(nc.sbuf_tensor("mur_sb", [128, NB], F32))
        muk_sb = E(nc.sbuf_tensor("muk_sb", [128, NB], F32))
        mur1_sb = E(nc.sbuf_tensor("mur1_sb", [128, NB], F32))
        muk1_sb = E(nc.sbuf_tensor("muk1_sb", [128, NB], F32))
        mu_sb = E(nc.sbuf_tensor("mu_sb", [128, 1], F32))
        mubc = E(nc.sbuf_tensor("mubc", [128, 1], F32))
        gamq_sb = E(nc.sbuf_tensor("gamq_sb", [128, NB], F32))
        msk_sb = E(nc.sbuf_tensor("msk_sb", [128, 1], F32))
        ln2b = E(nc.sbuf_tensor("ln2b", [128, 1], F32))
        zb = E(nc.sbuf_tensor("zb", [128, 1], F32))
        epsA = E(nc.sbuf_tensor("epsA", [128, 1], F32))
        epsB = E(nc.sbuf_tensor("epsB", [128, 1], F32))
        ones_r = E(nc.sbuf_tensor("ones_r", [1, 512], BF16))
        onesf_r = E(nc.sbuf_tensor("onesf_r", [1, 128], F32))
        onesf_c = E(nc.sbuf_tensor("onesf_c", [128, 1], F32))
        ones_cb = E(nc.sbuf_tensor("ones_cb", [128, 1], BF16))
        idf = E(nc.sbuf_tensor("idf", [128, 128], F32))
        idb = E(nc.sbuf_tensor("idb", [128, 128], BF16))
        bqe = E(nc.sbuf_tensor("bqe", [1, D], BF16))
        bke = E(nc.sbuf_tensor("bke", [1, D], BF16))
        bve = E(nc.sbuf_tensor("bve", [1, D], BF16))
        boe = E(nc.sbuf_tensor("boe", [1, D], BF16))
        xbn_sb = E(nc.sbuf_tensor("xbn_sb", [1, D], BF16))
        xbt_sb = E(nc.sbuf_tensor("xbt_sb", [128, NB], BF16))
        rowA = E(nc.sbuf_tensor("rowA", [1, 512], F32))
        rowB = E(nc.sbuf_tensor("rowB", [1, 512], F32))
        rowC = E(nc.sbuf_tensor("rowC", [1, 512], F32))
        rowD = E(nc.sbuf_tensor("rowD", [1, 512], F32))

        sems = {}
        for n in ("dS", "pe", "ac", "dv", "gp", "m1", "a2", "s1", "m2", "at"):
            sems[n] = Ctr(E(nc.semaphore("s_" + n)))
        dS = sems["dS"]
        pe, ac, dv, gp = sems["pe"], sems["ac"], sems["dv"], sems["gp"]
        m1, a2, s1, m2, at = sems["m1"], sems["a2"], sems["s1"], sems["m2"], sems["at"]
        dXs = [Ctr(E(nc.semaphore(f"s_dx{i}"))) for i in range(4)]
        dWs = [Ctr(E(nc.semaphore(f"s_dw{i}"))) for i in range(2)]
        dTs = [Ctr(E(nc.semaphore(f"s_dt{i}"))) for i in range(2)]
        dVs = [Ctr(E(nc.semaphore(f"s_dv2{i}"))) for i in range(2)]
        dKs = [Ctr(E(nc.semaphore(f"s_dk{i}"))) for i in range(2)]
        dQs = [Ctr(E(nc.semaphore(f"s_dq{i}"))) for i in range(2)]

        blk = E(nc.Block())

        def ch(eng, ctr, instr, n=1):
            ctr.inc(instr, n)
            eng.wait_ge(ctr.sem, ctr.v)

        @blk.sync
        def _(sy):
            for dst, src in ((gam_sb, gam_d), (bet_sb, bet_d), (mur_sb, mur_d),
                             (muk_sb, muk_d), (mu_sb, mu_d), (msk_sb, msk_d)):
                dS.inc(sy.dma_start(dst[:], src), 16)

        @blk.gpsimd
        def _(g):
            ch(g, gp, g.memset(idf[:], 0.0))
            ch(g, gp, g.affine_select(out=idf[:], in_=idf[:], compare_op=AX.not_equal,
                                      fill=1.0, base=0, pattern=[[-1, 128]],
                                      channel_multiplier=1))
            ch(g, gp, g.memset(idb[:], 0.0))
            ch(g, gp, g.affine_select(out=idb[:], in_=idb[:], compare_op=AX.not_equal,
                                      fill=1.0, base=0, pattern=[[-1, 128]],
                                      channel_multiplier=1))

        @blk.vector
        def _(v):
            v.memset(ones_r[:], 1.0)
            v.memset(onesf_r[:], 1.0)
            v.memset(onesf_c[:], 1.0)
            v.memset(ones_cb[:], 1.0)
            v.memset(ln2b[:], -LN2)
            v.memset(zb[:], 0.0)
            v.memset(epsA[:], 1e-6)
            ch(v, dv, v.memset(epsB[:], 1e-5))
            v.wait_ge(dS.sem, dS.v)
            ch(v, dv, v.tensor_scalar(mubc[:], mu_sb[:], -0.5, None, AX.mult))
            ch(v, dv, v.tensor_scalar(gamq_sb[:], gam_sb[:], 0.125, None, AX.mult))
            ch(v, dv, v.tensor_scalar(mur1_sb[:], mur_sb[:], -1.0, 1.0, AX.mult, AX.add))
            ch(v, dv, v.tensor_scalar(muk1_sb[:], muk_sb[:], -1.0, 1.0, AX.mult, AX.add))

        with ExitStack() as asc:
            EA = asc.enter_context
            wq_sb = EA(nc.sbuf_tensor("wq_sb", [128, NB, D], BF16))
            wk_sb = EA(nc.sbuf_tensor("wk_sb", [128, NB, D], BF16))
            wv_sb = EA(nc.sbuf_tensor("wv_sb", [128, NB, D], BF16))

            with ExitStack() as psc:
                EP = psc.enter_context
                wld = [EP(nc.sbuf_tensor(f"wld{i}", [128, D], F32)) for i in range(4)]
                wcs = [EP(nc.sbuf_tensor(f"wcs{i}", [128, D], BF16)) for i in range(2)]
                be1 = EP(nc.psum_tensor("be1", [1, 512], F32))
                be2 = EP(nc.psum_tensor("be2", [1, 512], F32))

                wplan = ["wq", "wk", "wv", "wo", "wr", "wkf", "wvf"]
                wsb = {"wq": wq_sb, "wk": wk_sb, "wv": wv_sb}
                bds_ = {"wq": bqe, "wk": bke, "wv": bve}
                brow_d = {"wq": b_in["bq"], "wk": b_in["bk"], "wv": b_in["bv"]}
                pe0, dv0 = pe.v, dv.v
                store_first = 3 * NB

                t_dG, t_pe, t_dv, t_st = {}, {}, {}, {}
                sem_of = {}
                cw = [dWs[0].v, dWs[1].v]
                cx = [dXs[0].v, dXs[1].v]
                ct = [dTs[0].v, dTs[1].v]
                cp, cv = pe0, dv0 + 2
                for wi, wn in enumerate(wplan):
                    for c2 in range(NB):
                        gi = wi * NB + c2
                        if gi % 2 == 0:
                            cw[(gi // 2) % 2] += 16
                            t_dG[gi] = cw[(gi // 2) % 2]
                            sem_of[gi] = dWs[(gi // 2) % 2]
                        else:
                            cx[(gi // 2) % 2] += 16
                            t_dG[gi] = cx[(gi // 2) % 2]
                            sem_of[gi] = dXs[(gi // 2) % 2]
                        if wn in ("wq", "wk", "wv"):
                            cp += 2
                        t_pe[gi] = cp
                        cv += 1
                        if wn in ("wq", "wk", "wv") and c2 == NB - 1:
                            cv += 4
                        t_dv[gi] = cv
                for wi, wn in enumerate(wplan):
                    if wi * NB >= store_first:
                        for c2 in range(NB):
                            gi = wi * NB + c2
                            ct[gi % 2] += 16
                            t_st[gi] = ct[gi % 2]
                wA_dv = cv

                bias_dS = {}

                def emit_ffn_store(g, gi):
                    g.wait_ge(dv.sem, t_dv[gi])
                    wi, c2 = gi // NB, gi % NB
                    dTs[gi % 2].inc(g.dma_start(wfd[wplan[wi]][c2], wcs[gi % 2][:]), 16)
                    assert dTs[gi % 2].v == t_st[gi]

                @blk.gpsimd
                def _(g):
                    for wi, wn in enumerate(wplan):
                        for c2 in range(NB):
                            gi = wi * NB + c2
                            if gi % 2 == 0:
                                if gi >= 4:
                                    g.wait_ge(dv.sem, t_dv[gi - 4])
                                    g.wait_ge(pe.sem, t_pe[gi - 4])
                                ins = g.dma_start(
                                    wld[gi % 4][:], w_in[wn].rearrange(
                                        "(b p) d -> b p d", p=128)[c2])
                                sem_of[gi].inc(ins, 16)
                                assert sem_of[gi].v == t_dG[gi]
                            if gi - 2 >= store_first:
                                emit_ffn_store(g, gi - 2)
                    for gi in (7 * NB - 2, 7 * NB - 1):
                        emit_ffn_store(g, gi)

                @blk.sync
                def _(sy):
                    sy.wait_ge(dS.sem, dS.v)
                    dS.inc(sy.dma_start(rowA[:], b_in["bo"][0:1, 0:512]), 16)
                    dS.inc(sy.dma_start(rowB[:], b_in["bo"][0:1, 512:1024]), 16)
                    bias_dS["bo"] = dS.v
                    rows_after = {7: ("wq", dv0 + 2), 15: ("wk", t_dv[7]),
                                  23: ("wv", t_dv[15])}
                    for wi, wn in enumerate(wplan):
                        for c2 in range(NB):
                            gi = wi * NB + c2
                            if gi % 2 == 1:
                                if gi >= 4:
                                    sy.wait_ge(dv.sem, t_dv[gi - 4])
                                    sy.wait_ge(pe.sem, t_pe[gi - 4])
                                ins = sy.dma_start(
                                    wld[gi % 4][:], w_in[wn].rearrange(
                                        "(b p) d -> b p d", p=128)[c2])
                                sem_of[gi].inc(ins, 16)
                                assert sem_of[gi].v == t_dG[gi]
                            if gi in rows_after:
                                rwn, rgate = rows_after[gi]
                                sy.wait_ge(dv.sem, rgate)
                                dS.inc(sy.dma_start(rowA[:],
                                                    brow_d[rwn][0:1, 0:512]), 16)
                                dS.inc(sy.dma_start(rowB[:],
                                                    brow_d[rwn][0:1, 512:1024]), 16)
                                bias_dS[rwn] = dS.v

                @blk.tensor
                def _(t):
                    t.wait_ge(dS.sem, bias_dS["bo"] - 32)
                    for wi, wn in enumerate(wplan):
                        if wn not in ("wq", "wk", "wv"):
                            continue
                        for c2 in range(NB):
                            gi = wi * NB + c2
                            t.wait_ge(sem_of[gi].sem, t_dG[gi])
                            if c2 == 0 and wi > 0:
                                t.wait_ge(dv.sem, t_dv[wi * NB - 1])
                            t.matmul(be1[0:1, :], bet_sb[:, c2:c2 + 1],
                                     wld[gi % 4][:, 0:512],
                                     start=(c2 == 0), stop=(c2 == NB - 1))
                            mm = t.matmul(be2[0:1, :], bet_sb[:, c2:c2 + 1],
                                          wld[gi % 4][:, 512:1024],
                                          start=(c2 == 0), stop=(c2 == NB - 1))
                            pe.inc(mm, 2)

                @blk.vector
                def _(v):
                    v.wait_ge(dS.sem, bias_dS["bo"])
                    ch(v, dv, v.tensor_copy(boe[0:1, 0:512], rowA[:]))
                    ch(v, dv, v.tensor_copy(boe[0:1, 512:1024], rowB[:]))
                    for wi, wn in enumerate(wplan):
                        for c2 in range(NB):
                            gi = wi * NB + c2
                            v.wait_ge(sem_of[gi].sem, t_dG[gi])
                            if wn in ("wq", "wk", "wv"):
                                v.wait_ge(pe.sem, t_pe[gi])
                                gsrc = gamq_sb if wn == "wq" else gam_sb
                                ins = v.tensor_scalar(wsb[wn][:, c2, :], wld[gi % 4][:],
                                                      gsrc[:, c2:c2 + 1], None, AX.mult)
                            else:
                                if gi - 2 in t_st:
                                    v.wait_ge(dTs[gi % 2].sem, t_st[gi - 2])
                                if wn == "wvf":
                                    ins = v.tensor_scalar(wcs[gi % 2][:], wld[gi % 4][:],
                                                          0.5, None, AX.mult)
                                else:
                                    ins = v.tensor_copy(wcs[gi % 2][:], wld[gi % 4][:])
                            ch(v, dv, ins)
                            if wn in ("wq", "wk", "wv") and c2 == NB - 1:
                                scl = 0.125 if wn == "wq" else 1.0
                                v.wait_ge(dS.sem, bias_dS[wn])
                                ch(v, dv, v.tensor_tensor(rowD[0:1, :], be1[0:1, :],
                                                          rowA[0:1, :], AX.add))
                                ch(v, dv, v.tensor_scalar(bds_[wn][0:1, 0:512],
                                                          rowD[0:1, :], scl, None, AX.mult))
                                ch(v, dv, v.tensor_tensor(rowD[0:1, :], be2[0:1, :],
                                                          rowB[0:1, :], AX.add))
                                ch(v, dv, v.tensor_scalar(bds_[wn][0:1, 512:1024],
                                                          rowD[0:1, :], scl, None, AX.mult))
                    assert dv.v == wA_dv, (dv.v, wA_dv)
                a0_pe_end = cp
                a0_dT_end = [dTs[0].v, dTs[1].v]

            with ExitStack() as psc:
                EP = psc.enter_context
                xg = [EP(nc.sbuf_tensor(f"xg{i}", [128, D], F32)) for i in range(4)]
                xn_tok = [EP(nc.sbuf_tensor(f"xntk{i}", [128, D], BF16)) for i in range(4)]
                xnt_g = EP(nc.sbuf_tensor("xnt_g", [128, NB, 512], BF16))
                kt_ev = [EP(nc.sbuf_tensor(f"ktev{i}", [128, 512], BF16)) for i in range(2)]
                v_ev = [EP(nc.sbuf_tensor(f"vev{i}", [128, NB, 132], BF16)) for i in range(2)]
                sumA = EP(nc.sbuf_tensor("sumA", [128, NT], F32))
                sqA = EP(nc.sbuf_tensor("sqA", [128, NT], F32))
                stm = EP(nc.sbuf_tensor("stm", [128, NT], F32))
                stv = EP(nc.sbuf_tensor("stv", [128, NT], F32))
                stn = EP(nc.sbuf_tensor("stn", [128, NT], F32))
                tr_ps = [EP(nc.psum_tensor(f"trp{i}", [128, 1024], BF16)) for i in range(2)]

                @blk.vector
                def _(v):
                    v.wait_ge(dv.sem, wA_dv)
                    v.wait_ge(pe.sem, a0_pe_end)
                    v.wait_ge(dTs[0].sem, a0_dT_end[0])
                    v.wait_ge(dTs[1].sem, a0_dT_end[1])
                    for i2 in range(2):
                        ch(v, dv, v.memset(v_ev[i2][:, :, 64:66], 1.0))
                        ch(v, dv, v.memset(v_ev[i2][:, :, 130:132], 1.0))
                a2_handover_dv = dv.v
                q1_scope = ExitStack()
                q1_ps = q1_scope.enter_context(nc.psum_tensor("q1p", [128, 8], F32))

                xr = xkv_d.rearrange("(n p) d -> n p d", p=128)

                @blk.sync
                def _(sy):
                    sy.wait_ge(dS.sem, dS.v)
                    sy.wait_ge(dv.sem, a2_handover_dv)
                    dS.inc(sy.dma_start(xg[0][0:1, :], xb_d), 16)

                @blk.scalar
                def _(s):
                    s.wait_ge(dS.sem, dS.v)
                    s.wait_ge(dv.sem, a2_handover_dv)
                    ch(s, ac, s.activation(xn_tok[0][0:1, :], xg[0][0:1, :], AF.Identity,
                                           accum_out=sumA[0:1, 0:1]))
                    ch(s, ac, s.activation(xn_tok[0][0:1, :], xg[0][0:1, :], AF.Square,
                                           accum_out=sqA[0:1, 0:1]))

                @blk.vector
                def _(v):
                    v.wait_ge(ac.sem, ac.v)
                    ch(v, dv, v.tensor_scalar(stm[0:1, 0:1], sumA[0:1, 0:1], 1.0 / D, None, AX.mult))
                    ch(v, dv, v.tensor_scalar(stv[0:1, 0:1], sqA[0:1, 0:1], 1.0 / D, None, AX.mult))
                    ch(v, dv, v.tensor_tensor(stn[0:1, 0:1], stm[0:1, 0:1], stm[0:1, 0:1], AX.mult))
                    ch(v, dv, v.tensor_sub(stv[0:1, 0:1], stv[0:1, 0:1], stn[0:1, 0:1]))

                @blk.scalar
                def _(s):
                    s.wait_ge(dv.sem, dv.v)
                    ch(s, ac, s.activation(stv[0:1, 0:1], stv[0:1, 0:1], AF.Ln, bias=epsA[0:1, :]))
                    ch(s, ac, s.activation(stv[0:1, 0:1], stv[0:1, 0:1], AF.Exp, scale=-0.5))

                @blk.vector
                def _(v):
                    v.wait_ge(ac.sem, ac.v)
                    ch(v, dv, v.scalar_tensor_tensor(stn[0:1, 0:1], stm[0:1, 0:1], -1.0,
                                                     stv[0:1, 0:1], op0=AX.mult, op1=AX.mult))
                    ch(v, dv, v.tensor_scalar(xbn_sb[0:1, :], xg[0][0:1, :],
                                              stv[0:1, 0:1], stn[0:1, 0:1], AX.mult, AX.add))

                @blk.tensor
                def _(t):
                    t.wait_ge(dv.sem, dv.v)
                    for b2 in range(NB):
                        mm = t.matmul(q1_ps[:, b2:b2 + 1],
                                      xbn_sb[0:1, 128 * b2:128 * (b2 + 1)],
                                      ones_r[0:1, 0:1], start=True, stop=True)
                    pe.inc(mm)

                @blk.vector
                def _(v):
                    v.wait_ge(pe.sem, pe.v)
                    ch(v, dv, v.tensor_copy(xbt_sb[:], q1_ps[:]))

                @blk.tensor
                def _(t):
                    t.wait_ge(dv.sem, dv.v)
                    for d2 in range(NB):
                        for c2 in range(NB):
                            t.matmul(q1_ps[:, d2:d2 + 1],
                                     wq_sb[:, c2, 128 * d2:128 * (d2 + 1)],
                                     xbt_sb[:, c2:c2 + 1], start=(c2 == 0), stop=False)
                        mm = t.matmul(q1_ps[:, d2:d2 + 1],
                                      bqe[0:1, 128 * d2:128 * (d2 + 1)],
                                      ones_r[0:1, 0:1], start=False, stop=True)
                    pe.inc(mm)

                @blk.vector
                def _(v):
                    v.wait_ge(pe.sem, pe.v)
                    ch(v, dv, v.tensor_copy(qt_sb[:, :, 0:1],
                                            q1_ps[:].rearrange("p (a o) -> p a o", o=1)))

                q1_scope.close()
                pj_ps = [EP(nc.psum_tensor(f"pjp{i}", [128, 512], F32)) for i in range(3)]
                kt_ps = [EP(nc.psum_tensor(f"ktp{i}", [128, 512], F32)) for i in range(3)]
                vN, vE, vVE, pR, pV, aS, aR, dX, gV = {}, {}, {}, {}, {}, {}, {}, {}, {}
                vVEh, vKE, vQE, pK, pQ, gK = {}, {}, {}, {}, {}, {}
                last_tr_evac = {}
                last_pj_evac = {}
                last_kt_bank = {}
                ktq_ctr = [0]

                def emit_sync(t):
                    @blk.sync
                    def _(sy, t=t):
                        if t < 4:
                            sy.wait_ge(dv.sem, a2_handover_dv)
                            sy.wait_ge(dv.sem, sweep0_dv)
                        if t >= 4:
                            sy.wait_ge(dv.sem, vNv[t - 4])
                        dXs[t % 4].inc(sy.dma_start(xg[t % 4][:], xr[t]), 16)
                        dX[t] = dXs[t % 4].v

                def emit_norm(t):
                    @blk.vector
                    def _(v, t=t):
                        ins = v.tensor_scalar(xn_tok[t % 4][:], xg[t % 4][:],
                                              stv[:, t:t + 1], stn[:, t:t + 1],
                                              AX.mult, AX.add)
                        dv.inc(ins)
                        vN[t] = dv.v
                        assert dv.v == vNv[t]

                def emit_transpose(t):
                    @blk.tensor
                    def _(tn, t=t):
                        tn.wait_ge(dv.sem, vN[t])
                        if t >= 2:
                            tn.wait_ge(ac.sem, vEa[t - 2])
                        for b2 in range(NB):
                            mm = tn.transpose(tr_ps[t % 2][:, 128 * b2:128 * (b2 + 1)],
                                              xn_tok[t % 4][:, 128 * b2:128 * (b2 + 1)],
                                              idb[:])
                        pe.inc(mm)
                        pR[t] = pe.v
                        assert pe.v == pRv[t]

                def emit_vproj(t):
                    @blk.tensor
                    def _(tn, t=t):
                        tn.wait_ge(ac.sem, vEa[t])
                        for hf in range(2):
                            idx = 2 * t + hf
                            bk = idx % 3
                            if idx >= 3:
                                tn.wait_ge(ac.sem, vVEa[(idx - 3) // 2])
                            for c2 in range(NB):
                                tn.matmul(pj_ps[bk][:],
                                          xnt_g[:, c2, 128 * (t % 4):128 * (t % 4 + 1)],
                                          wv_sb[:, c2, 512 * hf:512 * (hf + 1)],
                                          start=(c2 == 0), stop=False)
                            mm = tn.matmul(pj_ps[bk][:], ones_r[0:1, 0:128],
                                           bve[0:1, 512 * hf:512 * (hf + 1)],
                                           start=False, stop=True)
                            pe.inc(mm)
                        pV[t] = pe.v

                def emit_vstore(t):
                    @blk.gpsimd
                    def _(g, t=t):
                        g.wait_ge(ac.sem, vVEa[t])
                        for p2 in range(NB):
                            ins = g.dma_start(
                                vd[p2, t // 4, :, 132 * (t % 4):132 * (t % 4) + 132],
                                v_ev[t % 2][:, p2, :])
                            dVs[t % 2].inc(ins, 16)
                        gV[t] = dVs[t % 2].v
                        assert dVs[t % 2].v == gVv[t]

                def emit_ktproj(hh):
                    for d2 in range(NB):
                        ci2 = ktq_ctr[0]
                        ktq_ctr[0] += 1
                        bk = bk_of[("kt", hh, d2)]
                        assert bk == ci2 % 3

                        @blk.tensor
                        def _(tn, hh=hh, d2=d2, bk=bk, ci2=ci2):
                            tn.wait_ge(ac.sem, vEa[4 * hh + 3])
                            if ci2 >= 3:
                                tn.wait_ge(ac.sem, evA[ci2 - 3])
                            for c2 in range(NB):
                                tn.matmul(kt_ps[bk][:],
                                          wk_sb[:, c2, 128 * d2:128 * (d2 + 1)],
                                          xnt_g[:, c2, 0:512],
                                          start=(c2 == 0), stop=False)
                            mm = tn.matmul(kt_ps[bk][:],
                                           bke[0:1, 128 * d2:128 * (d2 + 1)],
                                           ones_r[0:1, 0:512], start=False, stop=True)
                            pe.inc(mm)
                            assert pe.v == pKv[(hh, d2)]

                        @blk.gpsimd
                        def _(g, hh=hh, d2=d2):
                            g.wait_ge(ac.sem, kEa[(hh, d2)])
                            ins = g.dma_start(ktd[128 * d2:128 * (d2 + 1),
                                                  512 * hh:512 * (hh + 1)],
                                              kt_ev[d2 % 2][:])
                            dKs[d2 % 2].inc(ins, 16)
                            assert dKs[d2 % 2].v == gKv[(hh, d2)]

                def emit_qproj(hh):
                    qw = min(512, TOKQ - 512 * hh)
                    for d2 in range(NB):
                        ci2 = ktq_ctr[0]
                        ktq_ctr[0] += 1
                        bk = bk_of[("q", hh, d2)]
                        assert bk == ci2 % 3

                        @blk.tensor
                        def _(tn, hh=hh, d2=d2, qw=qw, bk=bk, ci2=ci2):
                            tn.wait_ge(ac.sem, vEa[4 * hh + 3])
                            if ci2 >= 3:
                                tn.wait_ge(ac.sem, evA[ci2 - 3])
                            for c2 in range(NB):
                                tn.matmul(kt_ps[bk][:, 0:qw],
                                          wq_sb[:, c2, 128 * d2:128 * (d2 + 1)],
                                          xnt_g[:, c2, 0:qw],
                                          start=(c2 == 0), stop=False)
                            mm = tn.matmul(kt_ps[bk][:, 0:qw],
                                           bqe[0:1, 128 * d2:128 * (d2 + 1)],
                                           ones_r[0:1, 0:qw], start=False, stop=True)
                            pe.inc(mm)
                            assert pe.v == pQv[(hh, d2)]

                sweep0_dv = dv.v
                acB, dvB2, peB2 = ac.v, dv.v, pe.v
                dxB = [dXs[i].v for i in range(4)]
                NG4 = NT // 4
                dXv, aS2, aRg, chgv, pRv, vNv = {}, {}, {}, {}, {}, {}
                cvx, cpx = dvB2, peB2
                pVv, pKv, pQv, vEa = {}, {}, {}, {}
                kEa, qEa, bk_of, gKv = {}, {}, {}, {}
                ktq_seq = []
                ctr2 = 0
                for g in range(NG4):
                    for d2 in range(NB):
                        bk_of[("kt", g, d2)] = ctr2 % 3
                        ktq_seq.append(("kt", g, d2))
                        ctr2 += 1
                    if g in QHALVES:
                        for d2 in range(NB):
                            bk_of[("q", g, d2)] = ctr2 % 3
                            ktq_seq.append(("q", g, d2))
                            ctr2 += 1
                cdk = [dKs[0].v, dKs[1].v]
                for g in range(NG4):
                    for d2 in range(NB):
                        cdk[d2 % 2] += 16
                        gKv[(g, d2)] = cdk[d2 % 2]
                for g in range(NG4):
                    chgv[g] = cvx + 4
                    cvx += 5
                    for t in range(4 * g, 4 * g + 4):
                        cvx += 1
                        vNv[t] = cvx
                        cpx += 1
                        pRv[t] = cpx
                    for t in range(4 * g, 4 * g + 4):
                        cpx += 2
                        pVv[t] = cpx
                    hh = g
                    for d2 in range(NB):
                        cpx += 1
                        pKv[(hh, d2)] = cpx
                    if hh in QHALVES:
                        for d2 in range(NB):
                            cpx += 1
                            pQv[(hh, d2)] = cpx
                cax = acB
                vVEa = {}
                for g in range(NG4):
                    if g >= 1:
                        for t in range(4 * (g - 1), 4 * (g - 1) + 4):
                            cax += 2
                            vVEa[t] = cax
                    for t in range(4 * g, 4 * g + 4):
                        dXv[t] = dxB[t % 4] + 16 * (t // 4 + 1)
                        cax += 2
                        aS2[t] = cax
                    cax += 2
                    aRg[g] = cax
                    if g >= 1:
                        hp2 = g - 1
                        for d2 in range(NB):
                            cax += 1
                            kEa[(hp2, d2)] = cax
                        if hp2 in QHALVES:
                            for d2 in range(NB):
                                cax += 1
                                qEa[(hp2, d2)] = cax
                    for t in range(4 * g, 4 * g + 4):
                        cax += 1
                        vEa[t] = cax
                for t in range(4 * (NG4 - 1), 4 * (NG4 - 1) + 4):
                    cax += 2
                    vVEa[t] = cax
                hp2 = NG4 - 1
                for d2 in range(NB):
                    cax += 1
                    kEa[(hp2, d2)] = cax
                if hp2 in QHALVES:
                    for d2 in range(NB):
                        cax += 1
                        qEa[(hp2, d2)] = cax
                evA = [(kEa if kk == "kt" else qEa)[(hh_, d2_)]
                       for (kk, hh_, d2_) in ktq_seq]
                gVv = {}
                dvsB = [dVs[0].v, dVs[1].v]
                for t in range(NT):
                    gVv[t] = dvsB[t % 2] + 128 * (t // 2 + 1)

                @blk.scalar
                def _(s):
                    def emit_vevacs_act(s, gp):
                        for t in range(4 * gp, 4 * gp + 4):
                            for hf in range(2):
                                bkp = (2 * t + hf) % 3
                                s.wait_ge(pe.sem, pVv[t] - (1 - hf))
                                if t >= 2:
                                    s.wait_ge(dVs[t % 2].sem, gVv[t - 2])
                                dst = v_ev[t % 2][:, 4 * hf:4 * (hf + 1), :].rearrange(
                                    "p a (h x) -> p a h x", h=2)[:, :, :, 0:64]
                                srcp = pj_ps[bkp][:].rearrange(
                                    "p (a h x) -> p a h x", a=4, h=2)
                                ch(s, ac, s.activation(dst, srcp, AF.Identity))
                            assert ac.v == vVEa[t]
                    for g in range(NG4):
                        if g >= 1:
                            emit_vevacs_act(s, g - 1)
                        for t in range(4 * g, 4 * g + 4):
                            s.wait_ge(dXs[t % 4].sem, dXv[t])
                            if t == 0:
                                s.wait_ge(dv.sem, sweep0_dv)
                            if t >= 4:
                                s.wait_ge(pe.sem, pRv[t - 4])
                            ch(s, ac, s.activation(xn_tok[t % 4][:], xg[t % 4][:], AF.Identity,
                                                   accum_out=sumA[:, t:t + 1]))
                            ch(s, ac, s.activation(xn_tok[t % 4][:], xg[t % 4][:], AF.Square,
                                                   accum_out=sqA[:, t:t + 1]))
                            assert ac.v == aS2[t]
                        s.wait_ge(dv.sem, chgv[g])
                        ch(s, ac, s.activation(stv[:, 4 * g:4 * g + 4], stv[:, 4 * g:4 * g + 4],
                                               AF.Ln, bias=epsA[:]))
                        ch(s, ac, s.activation(stv[:, 4 * g:4 * g + 4], stv[:, 4 * g:4 * g + 4],
                                               AF.Exp, scale=-0.5))
                        assert ac.v == aRg[g]
                        if g >= 1:
                            emit_ktq_evacs(s, g - 1)
                        for t in range(4 * g, 4 * g + 4):
                            s.wait_ge(pe.sem, pRv[t])
                            if t >= 4:
                                t4 = t - 4
                                s.wait_ge(pe.sem, pVv[t4])
                                hh4 = t4 // 4
                                if (hh4, NB - 1) in pKv:
                                    s.wait_ge(pe.sem, pKv[(hh4, NB - 1)])
                                if (hh4, NB - 1) in pQv:
                                    s.wait_ge(pe.sem, pQv[(hh4, NB - 1)])
                            ch(s, ac, s.activation(
                                xnt_g[:, :, 128 * (t % 4):128 * (t % 4 + 1)],
                                tr_ps[t % 2][:].rearrange("p (b w) -> p b w", b=NB),
                                AF.Identity))
                            assert ac.v == vEa[t]
                        def emit_ktq_evacs(s, hp2):
                            for d2 in range(NB):
                                s.wait_ge(pe.sem, pKv[(hp2, d2)])
                                if d2 >= 2:
                                    s.wait_ge(dKs[d2 % 2].sem, gKv[(hp2, d2 - 2)])
                                elif hp2 >= 1:
                                    s.wait_ge(dKs[d2 % 2].sem,
                                              gKv[(hp2 - 1, d2 + NB - 2)])
                                ch(s, ac, s.activation(
                                    kt_ev[d2 % 2][:],
                                    kt_ps[bk_of[("kt", hp2, d2)]][:], AF.Identity))
                                assert ac.v == kEa[(hp2, d2)]
                            if hp2 in QHALVES:
                                lq2 = 512 * hp2 + 1
                                qw2 = min(512, TOKQ - 512 * hp2)
                                for d2 in range(NB):
                                    s.wait_ge(pe.sem, pQv[(hp2, d2)])
                                    ch(s, ac, s.activation(
                                        qt_sb[:, d2, lq2:lq2 + qw2],
                                        kt_ps[bk_of[("q", hp2, d2)]][:, 0:qw2],
                                        AF.Identity))
                                    assert ac.v == qEa[(hp2, d2)]
                    emit_vevacs_act(s, NG4 - 1)
                    emit_ktq_evacs(s, NG4 - 1)

                def emit_groupchain(g):
                    @blk.vector
                    def _(v, g=g):
                        sl = slice(4 * g, 4 * g + 4)
                        v.wait_ge(ac.sem, aS2[4 * g + 3])
                        ch(v, dv, v.tensor_scalar(stm[:, sl], sumA[:, sl], 1.0 / D, None, AX.mult))
                        ch(v, dv, v.tensor_scalar(stv[:, sl], sqA[:, sl], 1.0 / D, None, AX.mult))
                        ch(v, dv, v.tensor_tensor(stn[:, sl], stm[:, sl], stm[:, sl], AX.mult))
                        ch(v, dv, v.tensor_sub(stv[:, sl], stv[:, sl], stn[:, sl]))
                        assert dv.v == chgv[g]
                        v.wait_ge(ac.sem, aRg[g])
                        ch(v, dv, v.scalar_tensor_tensor(stn[:, sl], stm[:, sl], -1.0,
                                                         stv[:, sl], op0=AX.mult, op1=AX.mult))

                for g in range(NT // 4):
                    for t in range(4 * g, 4 * g + 4):
                        emit_sync(t)
                    emit_groupchain(g)
                    for t in range(4 * g, 4 * g + 4):
                        emit_norm(t)
                        emit_transpose(t)
                    for tt2 in range(4 * g, 4 * g + 4):
                        emit_vproj(tt2)
                        emit_vstore(tt2)
                    emit_ktproj(g)
                    if g in QHALVES:
                        emit_qproj(g)

                a_end_dv, a_end_pe, a_end_ac = dv.v, pe.v, ac.v
                a_end_dV = [dVs[0].v, dVs[1].v]
                a_end_dK = [dKs[0].v, dKs[1].v]
                a_end_dT = [dTs[0].v, dTs[1].v]
                a_end_dW = [dWs[0].v, dWs[1].v]
                a_end_dX = [dXs[i].v for i in range(4)]

        with ExitStack() as bscope:
            EB = bscope.enter_context
            ktb = [EB(nc.sbuf_tensor(f"ktb{i}", [128, 512], BF16)) for i in range(4)]
            vb = [EB(nc.sbuf_tensor(f"vbx{i}", [128, 528], BF16)) for i in range(4)]
            tbuf = [EB(nc.sbuf_tensor(f"tbf{i}", [128, 1024], BF16)) for i in range(WBUF)]
            ebuf = [EB(nc.sbuf_tensor(f"ebf{i}", [128, 1024], BF16)) for i in range(WBUF)]
            wbuf = [EB(nc.sbuf_tensor(f"wbf{i}", [128, 1024], BF16)) for i in range(WBUF)]
            xq2 = [EB(nc.sbuf_tensor(f"xq2{i}", [128, D], F32)) for i in range(2)]
            xt_sb = EB(nc.sbuf_tensor("xt_sb", [128, NB, 512], F32))
            sq_sb = EB(nc.sbuf_tensor("sq_sb", [128, 2, 512], F32))
            att_sb = EB(nc.sbuf_tensor("att_sb", [128, NB, 512], BF16))
            dinv = EB(nc.sbuf_tensor("dinv", [1, 1024], F32))
            dinvb = EB(nc.sbuf_tensor("dinvb", [64, 1024], F32))
            woh_sb = EB(nc.sbuf_tensor("woh_sb", [128, NB, 512], BF16))

            wo_prev_pe = [None]
            pend_epi = [None]
            prev_epi_done = [None]
            gpair_ctr = [0]
            xt_ready = [None]
            prev_xq_pe = [None]

            def emit_xt_prep(nqi, gate_dv, pb, sync_gate=()):
                Wn, qon = WL[nqi], QOFF[nqi]
                ntbn = (Wn + 127) // 128
                xdma_, tpv_, tev_ = {}, {}, {}
                for tbk in range(ntbn):
                    r = min(Wn - 128 * tbk, 128)

                    @blk.sync
                    def _(sy, tbk=tbk, pxq=prev_xq_pe[0], sg=tuple(sync_gate)):
                        if tbk == 0:
                            for sem_, val in sg:
                                sy.wait_ge(sem_, val)
                        if tbk == 0 and pxq is not None:
                            sy.wait_ge(pe.sem, pxq)
                        if tbk >= 2:
                            sy.wait_ge(pe.sem, tpv_[tbk - 2])
                        r0 = qon + 128 * tbk
                        r1 = min(qon + Wn, r0 + 128)
                        ss = dQs[tbk % 2]
                        if r0 == 0:
                            ss.inc(sy.dma_start(xq2[tbk % 2][0:1, :], xb_d), 16)
                            if r1 > 1:
                                ss.inc(sy.dma_start(xq2[tbk % 2][1:r1, :],
                                                    xkv_d[0:r1 - 1, :]), 16)
                        else:
                            ss.inc(sy.dma_start(xq2[tbk % 2][0:r1 - r0, :],
                                                xkv_d[r0 - 1:r1 - 1, :]), 16)
                        xdma_[tbk] = ss.v

                    @blk.tensor
                    def _(tn, tbk=tbk, r=r, gate_dv=gate_dv):
                        tn.wait_ge(dQs[tbk % 2].sem, xdma_[tbk])
                        if tbk < 2:
                            tn.wait_ge(dv.sem, gate_dv)
                        else:
                            tn.wait_ge(dv.sem, tev_[tbk - 2])
                        bA = pb[2 * (tbk % 2)]
                        bB = pb[2 * (tbk % 2) + 1]
                        for b2 in range(NB):
                            dst = bA if b2 < 4 else bB
                            cb = 128 * (b2 % 4)
                            mm = tn.transpose(dst[:, cb:cb + r],
                                              xq2[tbk % 2][0:r, 128 * b2:128 * (b2 + 1)],
                                              idf[0:r, 0:r])
                        pe.inc(mm)
                        tpv_[tbk] = pe.v

                    @blk.vector
                    def _(v, tbk=tbk, r=r):
                        v.wait_ge(pe.sem, tpv_[tbk])
                        bA = pb[2 * (tbk % 2)]
                        bB = pb[2 * (tbk % 2) + 1]
                        ch(v, dv, v.tensor_copy(
                            xt_sb[:, 0:4, 128 * tbk:128 * tbk + r],
                            bA[:].rearrange("p (b w) -> p b w", b=4)[:, :, 0:r]))
                        ch(v, dv, v.tensor_copy(
                            xt_sb[:, 4:8, 128 * tbk:128 * tbk + r],
                            bB[:].rearrange("p (b w) -> p b w", b=4)[:, :, 0:r]))
                    tev_[tbk] = dv.v
                prev_xq_pe[0] = tpv_[ntbn - 1]
                xt_ready[0] = tev_[ntbn - 1]

            with ExitStack() as prosc:
                xtp = [prosc.enter_context(nc.psum_tensor(f"xtp{i}", [128, 512], F32))
                       for i in range(4)]
                emit_xt_prep(0, a_end_dv, xtp,
                             sync_gate=[(pe.sem, a_end_pe), (dv.sem, a_end_dv),
                                        (ac.sem, a_end_ac)])

            @blk.tensor
            def _(tn, g0=xt_ready[0]):
                tn.wait_ge(dv.sem, g0)

            for qi, W in enumerate(WL):
                qo = QOFF[qi]
                with ExitStack() as qsc:
                    EQ = qsc.enter_context
                    st_ps = EQ(nc.psum_tensor(f"st_ps{qi}", [128, 2048], F32))
                    o_ps = [[EQ(nc.psum_tensor(f"o{h}_ps{qi}_{par}", [128, 512], F32))
                             for h in range(2)] for par in range(2)]

                    qm1B, qs1B = m1.v, s1.v

                    @blk.scalar
                    def _(s, W=W, qm1B=qm1B, qs1B=qs1B):
                        for p3 in range(NB):
                            m1B_, s1B_ = qm1B + KS * p3, qs1B + KS * p3
                            for k in range(KS):
                                s.wait_ge(m1.sem, m1B_ + k + 1)
                                s.wait_ge(s1.sem, max(0, s1B_ + k - WBUF + 1))
                                sgl = 1024 * (k % 2)
                                srcv = st_ps[:, sgl:sgl + 1024].rearrange(
                                    "p (h w) -> p h w", h=2)[:, :, 0:W]
                                ac.inc(s.activation(
                                    tbuf[k % WBUF][:].rearrange("p (h w) -> p h w", h=2)[:, :, 0:W],
                                    srcv, AF.Tanh, bias=mubc[:], scale=0.5))
                                a2.inc(s.activation(
                                    ebuf[k % WBUF][:].rearrange("p (h w) -> p h w", h=2)[:, :, 0:W],
                                    srcv, AF.Exp, bias=ln2b[:], scale=1.0))

                    qa2B = a2.v - NB * KS
                    for p2 in range(NB):
                        m1B, s1B, m2B = m1.v, s1.v, m2.v
                        a2B = qa2B + KS * p2
                        first = (qi == 0 and p2 == 0)
                        gpair = gpair_ctr[0]
                        kvk, kvv = {}, {}
                        epi = pend_epi[0]
                        dvB = dv.v
                        bcpe = {}

                        @blk.gpsimd
                        def _(g, p2=p2, m1B=m1B, first=first):
                            if first:
                                g.wait_ge(dv.sem, a_end_dv)
                                g.wait_ge(pe.sem, a_end_pe)
                                g.wait_ge(dKs[0].sem, a_end_dK[0])
                                g.wait_ge(dKs[1].sem, a_end_dK[1])
                                g.wait_ge(dWs[0].sem, a_end_dW[0])
                                g.wait_ge(dWs[1].sem, a_end_dW[1])
                                g.wait_ge(dTs[0].sem, a_end_dT[0])
                                g.wait_ge(dTs[1].sem, a_end_dT[1])
                            kdist = 4 if NCH >= 4 else NCH
                            for c2 in range(NCH):
                                g.wait_ge(m1.sem, max(0, m1B + 4 * c2 - 4 * kdist + 4))
                                ksem = (dWs + dKs)[c2 % 4]
                                ins = g.dma_start(
                                    ktb[c2 % 4][:],
                                    ktd.rearrange("(p r) s -> p r s", r=128)[p2][
                                        :, 512 * c2:512 * (c2 + 1)])
                                ksem.inc(ins, 16)
                                kvk[c2] = ksem.v

                        @blk.sync
                        def _(sy, p2=p2, m2B=m2B, first=first):
                            if first:
                                sy.wait_ge(dv.sem, a_end_dv)
                                sy.wait_ge(pe.sem, a_end_pe)
                                sy.wait_ge(dVs[0].sem, a_end_dV[0])
                                sy.wait_ge(dVs[1].sem, a_end_dV[1])
                                for i4 in range(4):
                                    sy.wait_ge(dXs[i4].sem, a_end_dX[i4])
                            kdist = 4 if NCH >= 4 else NCH
                            for c2 in range(NCH):
                                sy.wait_ge(m2.sem, max(0, m2B + 4 * c2 - 4 * kdist + 4))
                                dXs[c2 % 4].inc(sy.dma_start(vb[c2 % 4][:], vd[p2, c2]), 16)
                                kvv[c2] = dXs[c2 % 4].v

                        @blk.tensor
                        def _(tn, p2=p2, W=W, qo=qo, kvk=kvk, kvv=kvv, a2B=a2B,
                              s1B=s1B, first=first, gpair=gpair, epi=epi, dvB=dvB):
                            if first:
                                tn.wait_ge(dv.sem, a_end_dv)
                                tn.wait_ge(dTs[0].sem, a_end_dT[0])
                                tn.wait_ge(dTs[1].sem, a_end_dT[1])
                            par = gpair % 2
                            for k in range(KS + 2):
                                if k < KS:
                                    c2, j = k // 4, k % 4
                                    tn.wait_ge((dWs + dKs)[c2 % 4].sem, kvk[c2])
                                    tn.wait_ge(a2.sem, max(0, a2B + k - 1))
                                    sgl = 1024 * (k % 2)
                                    tn.matmul(st_ps[:, sgl:sgl + W],
                                              ktb[c2 % 4][0:64, 128 * j:128 * (j + 1)],
                                              qt_sb[0:64, p2, qo:qo + W],
                                              start=True, stop=True, tile_position=(0, 0))
                                    mm = tn.matmul(st_ps[:, sgl + 512:sgl + 512 + W],
                                                   ktb[c2 % 4][64:128, 128 * j:128 * (j + 1)],
                                                   qt_sb[64:128, p2, qo:qo + W],
                                                   start=True, stop=True, tile_position=(64, 0))
                                    m1.inc(mm)
                                if epi is not None and k == 3:
                                    opv, Wp = epi[0], epi[1]
                                    tn.wait_ge(dv.sem, dvB + 1)
                                    mmb = tn.matmul(opv[0][64:128, 0:Wp], onesf_r[0:1, 0:64],
                                                    dinv[0:1, 0:Wp], start=True, stop=True,
                                                    tile_position=(0, 64))
                                    bcpe[0] = pe.inc(mmb)
                                if epi is not None and k == 5:
                                    opv, Wp = epi[0], epi[1]
                                    tn.wait_ge(dv.sem, dvB + 2)
                                    mmb = tn.matmul(opv[1][64:128, 0:Wp], onesf_r[0:1, 0:64],
                                                    dinv[0:1, 512:512 + Wp], start=True, stop=True,
                                                    tile_position=(0, 64))
                                    bcpe[1] = pe.inc(mmb)
                                k2 = k - 2
                                if k2 >= 0:
                                    c2, j = k2 // 4, k2 % 4
                                    tn.wait_ge(s1.sem, s1B + k2 + 1)
                                    tn.wait_ge(dXs[c2 % 4].sem, kvv[c2])
                                    if k2 == 0:
                                        tn.wait_ge(at.sem, max(0, gpair - 1))
                                        if p2 == 0 and prev_epi_done[0] is not None:
                                            tn.wait_ge(dv.sem, prev_epi_done[0])
                                    tn.matmul(o_ps[par][0][0:65, 0:W],
                                              vb[c2 % 4][:, 132 * j:132 * j + 65],
                                              wbuf[k2 % WBUF][:, 0:W],
                                              start=(k2 == 0), stop=(k2 == KS - 1))
                                    mm = tn.matmul(o_ps[par][1][0:65, 0:W],
                                                   vb[c2 % 4][:, 132 * j + 66:132 * j + 131],
                                                   wbuf[k2 % WBUF][:, 512:512 + W],
                                                   start=(k2 == 0), stop=(k2 == KS - 1))
                                    m2.inc(mm)

                        @blk.vector
                        def _(v, W=W, a2B=a2B, m2B=m2B, epi=epi, dvB=dvB, gpair=gpair):
                            for k in range(KS):
                                v.wait_ge(a2.sem, a2B + k + 1)
                                v.wait_ge(m2.sem, max(0, m2B + k - WBUF + 1))
                                vw = lambda buf: buf.rearrange("p (h w) -> p h w", h=2)[:, :, 0:W]
                                s1.inc(v.scalar_tensor_tensor(
                                    vw(wbuf[k % WBUF][:]), vw(tbuf[k % WBUF][:]), 1.0,
                                    vw(ebuf[k % WBUF][:]), op0=AX.add, op1=AX.mult))
                                if epi is not None and k < 8:
                                    opv, Wp, pprev = epi[0], epi[1], epi[2]
                                    if k == 0:
                                        v.wait_ge(m2.sem, m2B)
                                        assert dv.v == dvB
                                        dv.inc(v.tensor_copy(dinv[0:1, 0:Wp],
                                                             opv[0][64:65, 0:Wp]))
                                        v.wait_ge(dv.sem, dv.v)
                                    elif k == 1:
                                        dv.inc(v.tensor_copy(dinv[0:1, 512:512 + Wp],
                                                             opv[1][64:65, 0:Wp]))
                                        v.wait_ge(dv.sem, dv.v)
                                    elif k == 4:
                                        v.wait_ge(pe.sem, bcpe[0])
                                        dv.inc(v.reciprocal(dinvb[:, 0:Wp],
                                                            opv[0][64:128, 0:Wp]))
                                        v.wait_ge(dv.sem, dv.v)
                                    elif k == 5:
                                        dv.inc(v.tensor_tensor(att_sb[0:64, pprev, 0:Wp],
                                                               opv[0][0:64, 0:Wp],
                                                               dinvb[:, 0:Wp], AX.mult))
                                    elif k == 6:
                                        v.wait_ge(pe.sem, bcpe[1])
                                        dv.inc(v.reciprocal(dinvb[:, 512:512 + Wp],
                                                            opv[1][64:128, 0:Wp]))
                                        v.wait_ge(dv.sem, dv.v)
                                    elif k == 7:
                                        ins = v.tensor_tensor(att_sb[64:128, pprev, 0:Wp],
                                                              opv[1][0:64, 0:Wp],
                                                              dinvb[:, 512:512 + Wp], AX.mult)
                                        at.inc(ins)

                        pend_epi[0] = (o_ps[gpair % 2], W, p2)
                        gpair_ctr[0] += 1

                    opv, Wp, pprev = pend_epi[0]
                    pend_epi[0] = None
                    tm2 = m2.v

                    @blk.vector
                    def _(v, opv=opv, Wp=Wp):
                        v.wait_ge(m2.sem, tm2)
                        ch(v, dv, v.tensor_copy(dinv[0:1, 0:Wp], opv[0][64:65, 0:Wp]))
                        ch(v, dv, v.tensor_copy(dinv[0:1, 512:512 + Wp], opv[1][64:65, 0:Wp]))
                    te0 = dv.v

                    @blk.tensor
                    def _(tn, opv=opv, Wp=Wp, te0=te0):
                        tn.wait_ge(dv.sem, te0)
                        tn.matmul(opv[0][64:128, 0:Wp], onesf_r[0:1, 0:64],
                                  dinv[0:1, 0:Wp], start=True, stop=True,
                                  tile_position=(0, 64))
                        mmb = tn.matmul(opv[1][64:128, 0:Wp], onesf_r[0:1, 0:64],
                                        dinv[0:1, 512:512 + Wp], start=True, stop=True,
                                        tile_position=(0, 64))
                        pe.inc(mmb)
                    te1 = pe.v

                    @blk.vector
                    def _(v, opv=opv, Wp=Wp, pprev=pprev, te1=te1):
                        v.wait_ge(pe.sem, te1)
                        ch(v, dv, v.reciprocal(dinvb[:, 0:Wp], opv[0][64:128, 0:Wp]))
                        ch(v, dv, v.reciprocal(dinvb[:, 512:512 + Wp], opv[1][64:128, 0:Wp]))
                        ch(v, dv, v.tensor_tensor(att_sb[0:64, pprev, 0:Wp],
                                                  opv[0][0:64, 0:Wp], dinvb[:, 0:Wp], AX.mult))
                        ins = v.tensor_tensor(att_sb[64:128, pprev, 0:Wp],
                                              opv[1][0:64, 0:Wp], dinvb[:, 512:512 + Wp], AX.mult)
                        at.inc(ins)
                        v.wait_ge(at.sem, at.v)

                with ExitStack() as esc:
                    EE = esc.enter_context
                    pp_ps = [EE(nc.psum_tensor(f"pp{qi}_{i}", [128, 512], F32)) for i in range(4)]
                    sm_ps = EE(nc.psum_tensor(f"sm_ps{qi}", [1, 512], F32))
                    sq_ps = EE(nc.psum_tensor(f"sq_ps{qi}", [1, 512], F32))
                    rb_ps = EE(nc.psum_tensor(f"rb_ps{qi}", [128, 512], F32))
                    nb_ps = EE(nc.psum_tensor(f"nb_ps{qi}", [128, 512], F32))

                    ntb = (W + 127) // 128
                    atE = at.v
                    xt_rd = xt_ready[0]

                    ppv, ypv, sqv2, stv2 = {}, {}, {}, {}
                    wo_wait = {}
                    for dh in range(2):
                        @blk.gpsimd
                        def _(g, dh=dh):
                            if wo_prev_pe[0] is not None:
                                g.wait_ge(pe.sem, wo_prev_pe[0])
                            g.wait_ge(dTs[dh].sem, dTs[dh].v)
                            for c2 in range(NB):
                                dTs[dh].inc(g.dma_start(
                                    woh_sb[:, c2, :],
                                    wfd["wo"][c2][:, 512 * dh:512 * (dh + 1)]), 16)
                            wo_wait[dh] = dTs[dh].v

                        for d2 in range(4 * dh, 4 * dh + 4):
                            @blk.tensor
                            def _(tn, d2=d2, dh=dh, W=W, atE=atE, xt_rd=xt_rd):
                                if d2 == 0:
                                    tn.wait_ge(at.sem, atE)
                                    tn.wait_ge(dv.sem, xt_rd)
                                if d2 == 4 * dh:
                                    tn.wait_ge(dTs[dh].sem, wo_wait[dh])
                                if d2 >= 4:
                                    tn.wait_ge(dv.sem, ypv[d2 - 4])
                                for c2 in range(NB):
                                    tn.matmul(pp_ps[d2 % 4][:, 0:W],
                                              woh_sb[:, c2, 128 * (d2 - 4 * dh):128 * (d2 - 4 * dh + 1)],
                                              att_sb[:, c2, 0:W], start=(c2 == 0), stop=False)
                                mm = tn.matmul(pp_ps[d2 % 4][:, 0:W],
                                               boe[0:1, 128 * d2:128 * (d2 + 1)],
                                               ones_r[0:1, 0:W], start=False, stop=True)
                                pe.inc(mm)
                                ppv[d2] = pe.v

                            @blk.vector
                            def _(v, d2=d2, W=W, qo=qo):
                                v.wait_ge(pe.sem, ppv[d2])
                                if d2 - 2 in stv2:
                                    v.wait_ge(pe.sem, stv2[d2 - 2])
                                ins = v.tensor_tensor(y_sb[:, d2, qo:qo + W],
                                                      pp_ps[d2 % 4][:, 0:W],
                                                      xt_sb[:, d2, 0:W], AX.add)
                                dv.inc(ins)
                                ypv[d2] = dv.v
                                v.wait_ge(dv.sem, dv.v)
                                ch(v, dv, v.tensor_tensor(sq_sb[:, d2 % 2, 0:W],
                                                          y_sb[:, d2, qo:qo + W],
                                                          y_sb[:, d2, qo:qo + W], AX.mult))
                                sqv2[d2] = dv.v

                            if d2 >= 1:
                                @blk.tensor
                                def _(tn, d2=d2, W=W, qo=qo):
                                    dd = d2 - 1
                                    tn.wait_ge(dv.sem, sqv2[dd])
                                    tn.matmul(sm_ps[0:1, 0:W], onesf_c[:],
                                              y_sb[:, dd, qo:qo + W],
                                              start=(dd == 0), stop=False)
                                    mm = tn.matmul(sq_ps[0:1, 0:W], onesf_c[:],
                                                   sq_sb[:, dd % 2, 0:W],
                                                   start=(dd == 0), stop=False)
                                    stv2[dd] = pe.inc(mm)
                        wo_prev_pe[0] = ppv[4 * dh + 3]

                    @blk.tensor
                    def _(tn, W=W, qo=qo):
                        dd = NB - 1
                        tn.wait_ge(dv.sem, sqv2[dd])
                        tn.matmul(sm_ps[0:1, 0:W], onesf_c[:], y_sb[:, dd, qo:qo + W],
                                  start=False, stop=True)
                        mm = tn.matmul(sq_ps[0:1, 0:W], onesf_c[:], sq_sb[:, dd % 2, 0:W],
                                       start=False, stop=True)
                        pe.inc(mm)
                    st_pe = pe.v

                    @blk.vector
                    def _(v, W=W, st_pe=st_pe):
                        v.wait_ge(pe.sem, st_pe)
                        ch(v, dv, v.tensor_scalar(rowA[0:1, 0:W], sm_ps[0:1, 0:W],
                                                  1.0 / D, None, AX.mult))
                        ch(v, dv, v.tensor_scalar(rowB[0:1, 0:W], sq_ps[0:1, 0:W],
                                                  1.0 / D, None, AX.mult))
                        ch(v, dv, v.tensor_tensor(rowD[0:1, 0:W], rowA[0:1, 0:W],
                                                  rowA[0:1, 0:W], AX.mult))
                        ch(v, dv, v.tensor_sub(rowB[0:1, 0:W], rowB[0:1, 0:W],
                                               rowD[0:1, 0:W]))
                    r1_dv = dv.v

                    if qi + 1 < len(WL):
                        emit_xt_prep(qi + 1, ypv[NB - 1], pp_ps)

                    @blk.scalar
                    def _(s, W=W, r1_dv=r1_dv):
                        s.wait_ge(dv.sem, r1_dv)
                        ch(s, ac, s.activation(rowB[0:1, 0:W], rowB[0:1, 0:W],
                                               AF.Ln, bias=epsB[0:1, :]))
                        ch(s, ac, s.activation(rowB[0:1, 0:W], rowB[0:1, 0:W],
                                               AF.Exp, scale=-0.5))
                    r1_ac = ac.v

                    @blk.vector
                    def _(v, W=W, r1_ac=r1_ac):
                        v.wait_ge(ac.sem, r1_ac)
                        ch(v, dv, v.scalar_tensor_tensor(rowC[0:1, 0:W], rowA[0:1, 0:W],
                                                         -1.0, rowB[0:1, 0:W],
                                                         op0=AX.mult, op1=AX.mult))
                    b1_dv = dv.v

                    @blk.tensor
                    def _(tn, W=W, b1_dv=b1_dv):
                        tn.wait_ge(dv.sem, b1_dv)
                        tn.matmul(rb_ps[:, 0:W], onesf_r[:], rowB[0:1, 0:W],
                                  start=True, stop=True)
                        pe.inc(tn.matmul(nb_ps[:, 0:W], onesf_r[:], rowC[0:1, 0:W],
                                         start=True, stop=True))
                    b1_pe = pe.v

                    @blk.vector
                    def _(v, W=W, qo=qo, qi=qi, b1_pe=b1_pe):
                        v.wait_ge(pe.sem, b1_pe)
                        for d2 in range(NB):
                            ch(v, dv, v.tensor_tensor(y_sb[:, d2, qo:qo + W],
                                                      y_sb[:, d2, qo:qo + W],
                                                      rb_ps[:, 0:W], AX.mult))
                            ch(v, dv, v.tensor_tensor(y_sb[:, d2, qo:qo + W],
                                                      y_sb[:, d2, qo:qo + W],
                                                      nb_ps[:, 0:W], AX.add))
                        if qi == 0:
                            ch(v, dv, v.tensor_scalar(y_sb[:, :, 0:1], y_sb[:, :, 0:1],
                                                      msk_sb[:], None, AX.mult))
                    nq_dv = dv.v
                    prev_epi_done[0] = nq_dv
                    bar_dv = xt_ready[0] if qi + 1 < len(WL) else nq_dv

                    @blk.tensor
                    def _(tn, bar_dv=bar_dv):
                        tn.wait_ge(dv.sem, bar_dv)

            b_end_dv, b_end_pe, b_end_ac = dv.v, pe.v, ac.v

        with ExitStack() as csc:
            EC = csc.enter_context
            wch = [EC(nc.sbuf_tensor(f"wch{i}", [128, D], BF16)) for i in range(8)]
            xr_sb = EC(nc.sbuf_tensor("xr_sb", [128, NB, 512], BF16))
            xk_sb = EC(nc.sbuf_tensor("xk_sb", [128, NB, 512], BF16))
            th_sb = EC(nc.sbuf_tensor("th_sb", [128, NB, 512], BF16))
            ka_sb = EC(nc.sbuf_tensor("ka_sb", [128, NB, 512], BF16))
            tmp_sb = EC(nc.sbuf_tensor("tmp_sb", [128, 512], F32))
            yp2_sb = EC(nc.sbuf_tensor("yp2", [128, NB, 512], F32))
            dump_sb = EC(nc.sbuf_tensor("dump_c", [128, D], F32))
            stat_sb = EC(nc.sbuf_tensor("stat_c", [128, 24], F32))
            out_sb = [EC(nc.sbuf_tensor(f"osb{i}", [128, D], F32)) for i in range(2)]

            NTC = len(CW)
            sl_sem = [dTs[0], dTs[1], dKs[0], dKs[1]]
            rw_name, rw_rhs = [], []
            for _ci in range(NTC):
                rw_name += ["wr", "wkf", "wvf"]
                rw_rhs += [xr_sb, xk_sb, ka_sb]

            with ExitStack() as psc:
                pr_ps = psc.enter_context(nc.psum_tensor("pr_ps", [128, NB, 512], F32))

                wv_ = {}
                slot_gate = {}
                mixv = {}
                halffree = {-2: (dv.sem, b_end_dv), -1: (dv.sem, b_end_dv)}
                reluA_dv, reluB_dv, sttA_dv, sttB_dv = {}, {}, {}, {}
                rAv, rBv, kAv, kBv, vAv, vBv = {}, {}, {}, {}, {}, {}
                trp = {}
                plan_id, plan_sq, plan_ln, plan_exp = {}, {}, {}, {}
                tanhA_p, tanhB_p = {}, {}
                varv, tsv = {}, {}
                osb_dma = {}
                osb_pred = {}
                first_loads = [True]
                TAILSEQ = [("id", 0), ("sq", 0), ("id", 1), ("sq", 1),
                           ("ln", 0), ("exp", 0), ("id", 2), ("sq", 2),
                           ("ln", 1), ("exp", 1), ("id", 3), ("sq", 3),
                           ("ln", 2), ("exp", 2), ("ln", 3), ("exp", 3)]

                def plan_tail(p, acB=None):
                    if acB is None:
                        acB = ac.v
                    ntb = CW[p] // 128
                    seq = [x for x in TAILSEQ if x[1] < ntb]
                    for i, (kind, k) in enumerate(seq):
                        {"id": plan_id, "sq": plan_sq,
                         "ln": plan_ln, "exp": plan_exp}[kind][(p, k)] = acB + i + 1
                    for k in range(ntb):
                        if k >= 2:
                            osb_pred[(p, k)] = dVs[k % 2].v + 16
                        else:
                            osb_pred[(p, k)] = osb_dma.get(k % 2)
                    return acB + len(seq)

                def emit_loads(r):
                    gates = dict(slot_gate)

                    @blk.gpsimd
                    def _(g, r=r, gates=gates, first=first_loads[0]):
                        if first:
                            g.wait_ge(dv.sem, b_end_dv)
                            g.wait_ge(pe.sem, b_end_pe)
                        for c in range(NB):
                            ss = sl_sem[c % 4]
                            g.wait_ge(ss.sem, ss.v)
                            if c in gates:
                                g.wait_ge(pe.sem, gates[c])
                            ss.inc(g.dma_start(wch[c][:], wfd[rw_name[r]][c]), 16)
                            wv_[(r, c)] = ss.v
                    first_loads[0] = False

                def wave(r, ci, wb, gates):
                    W = CW[ci]
                    h = ci % 2

                    @blk.tensor
                    def _(tn, r=r, W=W, h=h, wb=wb, gates=list(gates)):
                        for sem_, val in gates:
                            tn.wait_ge(sem_, val)
                        rhs = rw_rhs[r]
                        for c in range(NB):
                            if wb == 0:
                                tn.wait_ge(sl_sem[c % 4].sem, wv_[(r, c)])
                            for j in range(4):
                                mm = tn.matmul(
                                    pr_ps[:, 4 * h + j, 0:W],
                                    wch[c][:, 128 * (4 * wb + j):128 * (4 * wb + j + 1)],
                                    rhs[:, c, 0:W],
                                    start=(c == 0), stop=(c == NB - 1))
                            if wb == 1:
                                slot_gate[c] = pe.inc(mm)
                        if wb == 0:
                            pe.inc(mm)

                def emit_mix(mi):
                    @blk.vector
                    def _(v, mi=mi):
                        if mi == 0:
                            v.wait_ge(dv.sem, b_end_dv)
                            v.wait_ge(pe.sem, b_end_pe)
                        else:
                            v.wait_ge(pe.sem, kBv[mi - 1])
                        co, W = COFF[mi], CW[mi]
                        for d2 in range(NB):
                            ch(v, dv, v.tensor_sub(tmp_sb[:, 0:W],
                                                   y_sb[:, d2, co:co + W],
                                                   y_sb[:, d2, co - 1:co - 1 + W]))
                            ch(v, dv, v.scalar_tensor_tensor(
                                xr_sb[:, d2, 0:W], tmp_sb[:, 0:W],
                                mur_sb[:, d2:d2 + 1],
                                y_sb[:, d2, co - 1:co - 1 + W],
                                op0=AX.mult, op1=AX.add))
                            ch(v, dv, v.scalar_tensor_tensor(
                                xk_sb[:, d2, 0:W], tmp_sb[:, 0:W],
                                muk_sb[:, d2:d2 + 1],
                                y_sb[:, d2, co - 1:co - 1 + W],
                                op0=AX.mult, op1=AX.add))
                    mixv[mi] = dv.v

                def emit_relu(ci, wb):
                    @blk.vector
                    def _(v, ci=ci, wb=wb):
                        W, h = CW[ci], ci % 2
                        v.wait_ge(pe.sem, kAv[ci] if wb == 0 else kBv[ci])
                        for j in range(4):
                            ja = 4 * wb + j
                            ch(v, dv, v.tensor_scalar(tmp_sb[:, 0:W],
                                                      pr_ps[:, 4 * h + j, 0:W],
                                                      0.0, None, AX.max))
                            ch(v, dv, v.tensor_tensor(ka_sb[:, ja, 0:W],
                                                      pr_ps[:, 4 * h + j, 0:W],
                                                      tmp_sb[:, 0:W], AX.mult))
                    (reluA_dv if wb == 0 else reluB_dv)[ci] = dv.v

                def emit_stt(ci, wb):
                    @blk.vector
                    def _(v, ci=ci, wb=wb):
                        W, h, co = CW[ci], ci % 2, COFF[ci]
                        v.wait_ge(pe.sem, vAv[ci] if wb == 0 else vBv[ci])
                        v.wait_ge(ac.sem, tanhA_p[ci] if wb == 0 else tanhB_p[ci])
                        if wb == 0 and ci >= 1:
                            v.wait_ge(pe.sem, trp[ci - 1][CW[ci - 1] // 128 - 1])
                        for j in range(4):
                            ja = 4 * wb + j
                            ch(v, dv, v.scalar_tensor_tensor(
                                tmp_sb[:, 0:W], th_sb[:, ja, 0:W], 1.0,
                                pr_ps[:, 4 * h + j, 0:W],
                                op0=AX.add, op1=AX.mult))
                            ch(v, dv, v.tensor_tensor(yp2_sb[:, ja, 0:W],
                                                      tmp_sb[:, 0:W],
                                                      y_sb[:, ja, co:co + W], AX.add))
                    (sttA_dv if wb == 0 else sttB_dv)[ci] = dv.v

                def emit_tr(p, tbk):
                    @blk.tensor
                    def _(tn, p=p, tbk=tbk):
                        hp = p % 2
                        tn.wait_ge(dv.sem, sttB_dv[p])
                        if tbk >= 2:
                            tn.wait_ge(ac.sem, plan_sq[(p, tbk - 2)])
                        base = 4 * hp + 2 * (tbk % 2)
                        dstv = pr_ps[:, base:base + 2, :].rearrange("p a w -> p (a w)")
                        for b2 in range(NB):
                            mm = tn.transpose(dstv[:, 128 * b2:128 * (b2 + 1)],
                                              yp2_sb[:, b2, 128 * tbk:128 * (tbk + 1)],
                                              idf[:])
                        pe.inc(mm)
                    trp.setdefault(p, {})[tbk] = pe.v

                def emit_varchain(p, tbk):
                    @blk.vector
                    def _(v, p=p, tbk=tbk):
                        c0 = 6 * tbk
                        v.wait_ge(ac.sem, plan_sq[(p, tbk)])
                        ch(v, dv, v.tensor_scalar(stat_sb[:, c0 + 2:c0 + 3],
                                                  stat_sb[:, c0 + 0:c0 + 1],
                                                  1.0 / D, None, AX.mult))
                        ch(v, dv, v.tensor_scalar(stat_sb[:, c0 + 3:c0 + 4],
                                                  stat_sb[:, c0 + 1:c0 + 2],
                                                  1.0 / D, None, AX.mult))
                        ch(v, dv, v.tensor_tensor(stat_sb[:, c0 + 5:c0 + 6],
                                                  stat_sb[:, c0 + 2:c0 + 3],
                                                  stat_sb[:, c0 + 2:c0 + 3], AX.mult))
                        ch(v, dv, v.tensor_sub(stat_sb[:, c0 + 3:c0 + 4],
                                               stat_sb[:, c0 + 3:c0 + 4],
                                               stat_sb[:, c0 + 5:c0 + 6]))
                    varv[(p, tbk)] = dv.v

                def emit_nmr_ts(p, tbk):
                    @blk.vector
                    def _(v, p=p, tbk=tbk):
                        c0 = 6 * tbk
                        v.wait_ge(ac.sem, plan_exp[(p, tbk)])
                        ch(v, dv, v.scalar_tensor_tensor(stat_sb[:, c0 + 5:c0 + 6],
                                                         stat_sb[:, c0 + 2:c0 + 3],
                                                         -1.0,
                                                         stat_sb[:, c0 + 4:c0 + 5],
                                                         op0=AX.mult, op1=AX.mult))
                        ch(v, dv, v.tensor_scalar(out_sb[tbk % 2][:],
                                                  out_sb[tbk % 2][:],
                                                  stat_sb[:, c0 + 4:c0 + 5],
                                                  stat_sb[:, c0 + 5:c0 + 6],
                                                  AX.mult, AX.add))
                    tsv[(p, tbk)] = dv.v

                def emit_tail_act(p, ci, planned_end):
                    ntb = CW[p] // 128 if p is not None else 0
                    seq = [x for x in TAILSEQ if x[1] < ntb] if p is not None else []

                    @blk.scalar
                    def _(s, p=p, ci=ci, seq=list(seq)):
                        hp = p % 2 if p is not None else 0
                        if ci is not None:
                            W, h = CW[ci], ci % 2
                            s.wait_ge(pe.sem, rAv[ci])
                            if ci >= 1:
                                s.wait_ge(dv.sem, sttA_dv[ci - 1])
                            for j in range(4):
                                ins = s.activation(th_sb[:, j, 0:W],
                                                   pr_ps[:, 4 * h + j, 0:W],
                                                   AF.Tanh, bias=zb[:], scale=0.5)
                            ac.inc(ins)
                            s.wait_ge(pe.sem, rBv[ci])
                            if ci >= 1:
                                s.wait_ge(dv.sem, sttB_dv[ci - 1])
                            for j in range(4):
                                ins = s.activation(th_sb[:, 4 + j, 0:W],
                                                   pr_ps[:, 4 * h + j, 0:W],
                                                   AF.Tanh, bias=zb[:], scale=0.5)
                            ac.inc(ins)
                        for kind, k in seq:
                            c0 = 6 * k
                            base = 4 * hp + 2 * (k % 2)
                            dstv = pr_ps[:, base:base + 2, :].rearrange(
                                "p a w -> p (a w)")
                            if kind == "id":
                                s.wait_ge(pe.sem, trp[p][k])
                                if osb_pred[(p, k)] is not None:
                                    s.wait_ge(dVs[k % 2].sem, osb_pred[(p, k)])
                                ch(s, ac, s.activation(out_sb[k % 2][:], dstv,
                                                       AF.Identity,
                                                       accum_out=stat_sb[:, c0:c0 + 1]))
                            elif kind == "sq":
                                ch(s, ac, s.activation(dump_sb[:], dstv, AF.Square,
                                                       accum_out=stat_sb[:, c0 + 1:c0 + 2]))
                            elif kind == "ln":
                                s.wait_ge(dv.sem, varv[(p, k)])
                                ch(s, ac, s.activation(stat_sb[:, c0 + 4:c0 + 5],
                                                       stat_sb[:, c0 + 3:c0 + 4],
                                                       AF.Ln, bias=epsB[:]))
                            else:
                                ch(s, ac, s.activation(stat_sb[:, c0 + 4:c0 + 5],
                                                       stat_sb[:, c0 + 4:c0 + 5],
                                                       AF.Exp, scale=-0.5))
                    assert ac.v == planned_end, (ac.v, planned_end)

                def emit_store(p, tbk, fo):
                    pred = osb_pred.get((p, tbk + 2))
                    if pred is not None:
                        assert pred == dVs[tbk % 2].v + 16, (pred, dVs[tbk % 2].v)

                    @blk.gpsimd
                    def _(g, p=p, tbk=tbk, fo=fo):
                        co = COFF[p]
                        g.wait_ge(dv.sem, tsv[(p, tbk)])
                        if fo:
                            g.wait_ge(dVs[tbk % 2].sem, dVs[tbk % 2].v)
                        dVs[tbk % 2].inc(g.dma_start(
                            out_d[co - 1 + 128 * tbk:co - 1 + 128 * (tbk + 1), :],
                            out_sb[tbk % 2][:]), 16)
                        osb_dma[tbk % 2] = dVs[tbk % 2].v

                emit_mix(0)
                emit_loads(0)
                for ci in range(NTC):
                    prev = ci - 1 if ci >= 1 else None
                    ac0 = ac.v
                    tanhA_p[ci] = ac0 + 1
                    tanhB_p[ci] = ac0 + 2
                    if prev is not None:
                        pend = plan_tail(prev, acB=ac0 + 2)
                    else:
                        pend = ac0 + 2
                    if prev is not None:
                        for tb in range(min(2, CW[prev] // 128)):
                            emit_tr(prev, tb)
                    wave(3 * ci, ci, 0, [(dv.sem, mixv[ci]), halffree[ci - 2]])
                    rAv[ci] = pe.v
                    wave(3 * ci, ci, 1, [(ac.sem, tanhA_p[ci])])
                    rBv[ci] = pe.v
                    if prev is not None:
                        for tb in range(2, CW[prev] // 128):
                            emit_tr(prev, tb)
                    emit_loads(3 * ci + 1)
                    if prev is not None:
                        for tb in range(CW[prev] // 128):
                            emit_varchain(prev, tb)
                            emit_nmr_ts(prev, tb)
                            emit_store(prev, tb, fo=(prev == 0 and tb < 2))
                        halffree[prev] = (ac.sem,
                                          plan_sq[(prev, CW[prev] // 128 - 1)])
                    wave(3 * ci + 1, ci, 0, [(ac.sem, tanhB_p[ci])])
                    kAv[ci] = pe.v
                    emit_relu(ci, 0)
                    wave(3 * ci + 1, ci, 1, [(dv.sem, reluA_dv[ci])])
                    kBv[ci] = pe.v
                    emit_loads(3 * ci + 2)
                    if ci + 1 < NTC:
                        emit_mix(ci + 1)
                    emit_relu(ci, 1)
                    wave(3 * ci + 2, ci, 0, [(dv.sem, reluB_dv[ci])])
                    vAv[ci] = pe.v
                    emit_stt(ci, 0)
                    wave(3 * ci + 2, ci, 1, [(dv.sem, sttA_dv[ci])])
                    vBv[ci] = pe.v
                    emit_stt(ci, 1)
                    emit_tail_act(prev, ci, pend)
                    if ci + 1 < NTC:
                        emit_loads(3 * (ci + 1))

                p = NTC - 1
                pend = plan_tail(p)
                for tb in range(CW[p] // 128):
                    emit_tr(p, tb)
                for tb in range(CW[p] // 128):
                    emit_varchain(p, tb)
                    emit_nmr_ts(p, tb)
                emit_tail_act(p, None, pend)
                for tb in range(CW[p] // 128):
                    emit_store(p, tb, fo=(p == 0 and tb < 2))

        @blk.gpsimd
        def _(g):
            g.wait_ge(dVs[0].sem, dVs[0].v)
            g.wait_ge(dVs[1].sem, dVs[1].v)

    return nc


_CACHE = {}


def kernel(L0_attn_sequence, L0_attn_ln_gamma, L0_attn_ln_beta,
           L0_attn_Wq, L0_attn_bq, L0_attn_Wk, L0_attn_bk,
           L0_attn_Wv, L0_attn_bv, L0_attn_Wo, L0_attn_bo, L0_attn_mu,
           L0_ffn_W_r, L0_ffn_W_k, L0_ffn_W_v, L0_ffn_mu_r, L0_ffn_mu_k):
    x = np.ascontiguousarray(np.asarray(L0_attn_sequence, dtype=np.float32))
    B, S, _ = x.shape
    TOKQ = S // 4
    NCORE = 8

    key = (S, TOKQ)
    if key not in _CACHE:
        _CACHE[key] = build(S, TOKQ)
    nc = _CACHE[key]

    def row(a):
        return np.ascontiguousarray(np.asarray(a, dtype=np.float32)).reshape(1, D)

    def col8(a):
        return np.ascontiguousarray(np.asarray(a, dtype=np.float32).reshape(NB, 128).T)

    wT = {n: np.ascontiguousarray(np.asarray(w, dtype=np.float32).T) for n, w in
          (("wq", L0_attn_Wq), ("wk", L0_attn_Wk), ("wv", L0_attn_Wv),
           ("wo", L0_attn_Wo), ("wr", L0_ffn_W_r), ("wkf", L0_ffn_W_k),
           ("wvf", L0_ffn_W_v))}
    shared = dict(wT)
    shared.update({"bq": row(L0_attn_bq), "bk": row(L0_attn_bk),
                   "bv": row(L0_attn_bv), "bo": row(L0_attn_bo),
                   "gam": col8(L0_attn_ln_gamma), "bet": col8(L0_attn_ln_beta),
                   "mur": col8(L0_ffn_mu_r), "muk": col8(L0_ffn_mu_k),
                   "mu": np.full((128, 1), float(np.asarray(L0_attn_mu).reshape(())),
                                 dtype=np.float32)})

    in_maps = []
    for c in range(NCORE):
        b = c // (NCORE // B)
        t0 = TOKQ * (c % (NCORE // B))
        if t0 == 0:
            xkv = x[b]
            xb = x[b, 0:1]
            msk = np.zeros((128, 1), dtype=np.float32)
        else:
            xkv = np.ascontiguousarray(
                np.concatenate([x[b, t0:], x[b, :t0]], axis=0))
            xb = np.ascontiguousarray(x[b, t0 - 1:t0])
            msk = np.ones((128, 1), dtype=np.float32)
        m = dict(shared)
        m.update({"xkv": xkv, "xb": xb, "msk": msk})
        in_maps.append(m)

    res = run_bass_kernel_spmd(nc, in_maps, core_ids=list(range(NCORE)))
    out = np.empty((B, S, D), dtype=np.float32)
    for c in range(NCORE):
        b = c // (NCORE // B)
        t0 = TOKQ * (c % (NCORE // B))
        out[b, t0:t0 + TOKQ] = res.results[c]["out"]
    return out


# version 21 (final)
# speedup vs baseline: 1.0348x; 1.0348x over previous
import numpy as np
from contextlib import ExitStack

import concourse.bass as bass
import concourse.mybir as mybir
from concourse.bass_utils import run_bass_kernel_spmd

F32 = mybir.dt.float32
BF16 = mybir.dt.bfloat16
AX = mybir.AluOpType
AF = mybir.ActivationFunctionType
LN2 = float(np.log(2.0))

D = 1024
NB = 8
WBUF = 4


def qt_widths(nq):
    n = (nq + 511) // 512
    base = nq // n
    extra = nq - base * n
    return [base + 1] * extra + [base] * (n - extra)


class Ctr:
    __slots__ = ("sem", "v")

    def __init__(self, sem):
        self.sem = sem
        self.v = 0

    def inc(self, instr, n=1):
        instr.then_inc(self.sem, n)
        self.v += n
        return self.v


def build(S, TOKQ):
    NQ = TOKQ + 1
    KS = S // 128
    NCH = S // 512
    NT = S // 128
    WL = qt_widths(NQ)
    QOFF = [sum(WL[:i]) for i in range(len(WL))]
    NQP = NQ + (-NQ) % 4
    CW = [512] * (TOKQ // 512) + ([TOKQ % 512] if TOKQ % 512 else [])
    COFF = [1 + 512 * i for i in range(len(CW))]
    QHALVES = list(range((TOKQ + 511) // 512))

    nc = bass.Bass(target_bir_lowering=False, trn_type="TRN2")

    xkv_d = nc.dram_tensor("xkv", [S, D], F32, kind="ExternalInput").ap()
    xb_d = nc.dram_tensor("xb", [1, D], F32, kind="ExternalInput").ap()
    w_in = {w: nc.dram_tensor(w, [D, D], F32, kind="ExternalInput").ap()
            for w in ("wq", "wk", "wv", "wo", "wr", "wkf", "wvf")}
    b_in = {b: nc.dram_tensor(b, [1, D], F32, kind="ExternalInput").ap()
            for b in ("bq", "bk", "bv", "bo")}
    gam_d = nc.dram_tensor("gam", [128, NB], F32, kind="ExternalInput").ap()
    bet_d = nc.dram_tensor("bet", [128, NB], F32, kind="ExternalInput").ap()
    mur_d = nc.dram_tensor("mur", [128, NB], F32, kind="ExternalInput").ap()
    muk_d = nc.dram_tensor("muk", [128, NB], F32, kind="ExternalInput").ap()
    mu_d = nc.dram_tensor("mu", [128, 1], F32, kind="ExternalInput").ap()
    msk_d = nc.dram_tensor("msk", [128, 1], F32, kind="ExternalInput").ap()
    out_d = nc.dram_tensor("out", [TOKQ, D], F32, kind="ExternalOutput").ap()
    ktd = nc.dram_tensor("ktd", [D, S], BF16, kind="Internal").ap()
    vd = nc.dram_tensor("vd", [NB, NCH, 128, 528], BF16, kind="Internal").ap()
    wfd = {w: nc.dram_tensor(w + "s", [NB, 128, D], BF16, kind="Internal").ap()
           for w in ("wr", "wkf", "wvf", "wo")}

    with ExitStack() as big:
        E = big.enter_context
        qt_sb = E(nc.sbuf_tensor("qt_sb", [128, NB, NQP], BF16))
        y_sb = E(nc.sbuf_tensor("y_sb", [128, NB, NQ], F32))
        gam_sb = E(nc.sbuf_tensor("gam_sb", [128, NB], F32))
        bet_sb = E(nc.sbuf_tensor("bet_sb", [128, NB], F32))
        mur_sb = E(nc.sbuf_tensor("mur_sb", [128, NB], F32))
        muk_sb = E(nc.sbuf_tensor("muk_sb", [128, NB], F32))
        mur1_sb = E(nc.sbuf_tensor("mur1_sb", [128, NB], F32))
        muk1_sb = E(nc.sbuf_tensor("muk1_sb", [128, NB], F32))
        mu_sb = E(nc.sbuf_tensor("mu_sb", [128, 1], F32))
        mubc = E(nc.sbuf_tensor("mubc", [128, 1], F32))
        gamq_sb = E(nc.sbuf_tensor("gamq_sb", [128, NB], F32))
        msk_sb = E(nc.sbuf_tensor("msk_sb", [128, 1], F32))
        ln2b = E(nc.sbuf_tensor("ln2b", [128, 1], F32))
        zb = E(nc.sbuf_tensor("zb", [128, 1], F32))
        epsA = E(nc.sbuf_tensor("epsA", [128, 1], F32))
        epsB = E(nc.sbuf_tensor("epsB", [128, 1], F32))
        ones_r = E(nc.sbuf_tensor("ones_r", [1, 512], BF16))
        onesf_r = E(nc.sbuf_tensor("onesf_r", [1, 128], F32))
        onesf_c = E(nc.sbuf_tensor("onesf_c", [128, 1], F32))
        ones_cb = E(nc.sbuf_tensor("ones_cb", [128, 1], BF16))
        idf = E(nc.sbuf_tensor("idf", [128, 128], F32))
        idb = E(nc.sbuf_tensor("idb", [128, 128], BF16))
        bqe = E(nc.sbuf_tensor("bqe", [1, D], BF16))
        bke = E(nc.sbuf_tensor("bke", [1, D], BF16))
        bve = E(nc.sbuf_tensor("bve", [1, D], BF16))
        boe = E(nc.sbuf_tensor("boe", [1, D], BF16))
        xbn_sb = E(nc.sbuf_tensor("xbn_sb", [1, D], BF16))
        xbt_sb = E(nc.sbuf_tensor("xbt_sb", [128, NB], BF16))
        rowA = E(nc.sbuf_tensor("rowA", [1, 512], F32))
        rowB = E(nc.sbuf_tensor("rowB", [1, 512], F32))
        rowC = E(nc.sbuf_tensor("rowC", [1, 512], F32))
        rowD = E(nc.sbuf_tensor("rowD", [1, 512], F32))

        sems = {}
        for n in ("dS", "pe", "ac", "dv", "gp", "m1", "a2", "s1", "m2", "at"):
            sems[n] = Ctr(E(nc.semaphore("s_" + n)))
        dS = sems["dS"]
        pe, ac, dv, gp = sems["pe"], sems["ac"], sems["dv"], sems["gp"]
        m1, a2, s1, m2, at = sems["m1"], sems["a2"], sems["s1"], sems["m2"], sems["at"]
        dXs = [Ctr(E(nc.semaphore(f"s_dx{i}"))) for i in range(4)]
        dWs = [Ctr(E(nc.semaphore(f"s_dw{i}"))) for i in range(2)]
        dTs = [Ctr(E(nc.semaphore(f"s_dt{i}"))) for i in range(2)]
        dVs = [Ctr(E(nc.semaphore(f"s_dv2{i}"))) for i in range(2)]
        dKs = [Ctr(E(nc.semaphore(f"s_dk{i}"))) for i in range(2)]
        dQs = [Ctr(E(nc.semaphore(f"s_dq{i}"))) for i in range(2)]

        blk = E(nc.Block())

        def ch(eng, ctr, instr, n=1):
            ctr.inc(instr, n)
            eng.wait_ge(ctr.sem, ctr.v)

        @blk.sync
        def _(sy):
            for dst, src in ((gam_sb, gam_d), (bet_sb, bet_d), (mur_sb, mur_d),
                             (muk_sb, muk_d), (mu_sb, mu_d), (msk_sb, msk_d)):
                dS.inc(sy.dma_start(dst[:], src), 16)

        @blk.gpsimd
        def _(g):
            ch(g, gp, g.memset(idf[:], 0.0))
            ch(g, gp, g.affine_select(out=idf[:], in_=idf[:], compare_op=AX.not_equal,
                                      fill=1.0, base=0, pattern=[[-1, 128]],
                                      channel_multiplier=1))
            ch(g, gp, g.memset(idb[:], 0.0))
            ch(g, gp, g.affine_select(out=idb[:], in_=idb[:], compare_op=AX.not_equal,
                                      fill=1.0, base=0, pattern=[[-1, 128]],
                                      channel_multiplier=1))

        @blk.vector
        def _(v):
            v.memset(ones_r[:], 1.0)
            v.memset(onesf_r[:], 1.0)
            v.memset(onesf_c[:], 1.0)
            v.memset(ones_cb[:], 1.0)
            v.memset(ln2b[:], -LN2)
            v.memset(zb[:], 0.0)
            v.memset(epsA[:], 1e-6)
            ch(v, dv, v.memset(epsB[:], 1e-5))
            v.wait_ge(dS.sem, dS.v)
            ch(v, dv, v.tensor_scalar(mubc[:], mu_sb[:], -0.5, None, AX.mult))
            ch(v, dv, v.tensor_scalar(gamq_sb[:], gam_sb[:], 0.125, None, AX.mult))
            ch(v, dv, v.tensor_scalar(mur1_sb[:], mur_sb[:], -1.0, 1.0, AX.mult, AX.add))
            ch(v, dv, v.tensor_scalar(muk1_sb[:], muk_sb[:], -1.0, 1.0, AX.mult, AX.add))

        with ExitStack() as asc:
            EA = asc.enter_context
            wq_sb = EA(nc.sbuf_tensor("wq_sb", [128, NB, D], BF16))
            wk_sb = EA(nc.sbuf_tensor("wk_sb", [128, NB, D], BF16))
            wv_sb = EA(nc.sbuf_tensor("wv_sb", [128, NB, D], BF16))

            with ExitStack() as psc:
                EP = psc.enter_context
                wld = [EP(nc.sbuf_tensor(f"wld{i}", [128, D], F32)) for i in range(4)]
                wcs = [EP(nc.sbuf_tensor(f"wcs{i}", [128, D], BF16)) for i in range(2)]
                be1 = EP(nc.psum_tensor("be1", [1, 512], F32))
                be2 = EP(nc.psum_tensor("be2", [1, 512], F32))

                wplan = ["wq", "wk", "wv", "wo", "wr", "wkf", "wvf"]
                wsb = {"wq": wq_sb, "wk": wk_sb, "wv": wv_sb}
                bds_ = {"wq": bqe, "wk": bke, "wv": bve}
                brow_d = {"wq": b_in["bq"], "wk": b_in["bk"], "wv": b_in["bv"]}
                pe0, dv0 = pe.v, dv.v
                store_first = 3 * NB

                t_dG, t_pe, t_dv, t_st = {}, {}, {}, {}
                sem_of = {}
                cw = [dWs[0].v, dWs[1].v]
                cx = [dXs[0].v, dXs[1].v]
                ct = [dTs[0].v, dTs[1].v]
                cp, cv = pe0, dv0 + 2
                for wi, wn in enumerate(wplan):
                    for c2 in range(NB):
                        gi = wi * NB + c2
                        if gi % 2 == 0:
                            cw[(gi // 2) % 2] += 16
                            t_dG[gi] = cw[(gi // 2) % 2]
                            sem_of[gi] = dWs[(gi // 2) % 2]
                        else:
                            cx[(gi // 2) % 2] += 16
                            t_dG[gi] = cx[(gi // 2) % 2]
                            sem_of[gi] = dXs[(gi // 2) % 2]
                        if wn in ("wq", "wk", "wv"):
                            cp += 2
                        t_pe[gi] = cp
                        cv += 1
                        if wn in ("wq", "wk", "wv") and c2 == NB - 1:
                            cv += 4
                        t_dv[gi] = cv
                for wi, wn in enumerate(wplan):
                    if wi * NB >= store_first:
                        for c2 in range(NB):
                            gi = wi * NB + c2
                            ct[gi % 2] += 16
                            t_st[gi] = ct[gi % 2]
                wA_dv = cv

                bias_dS = {}

                def emit_ffn_store(g, gi):
                    g.wait_ge(dv.sem, t_dv[gi])
                    wi, c2 = gi // NB, gi % NB
                    dTs[gi % 2].inc(g.dma_start(wfd[wplan[wi]][c2], wcs[gi % 2][:]), 16)
                    assert dTs[gi % 2].v == t_st[gi]

                @blk.gpsimd
                def _(g):
                    for wi, wn in enumerate(wplan):
                        for c2 in range(NB):
                            gi = wi * NB + c2
                            if gi % 2 == 0:
                                if gi >= 4:
                                    g.wait_ge(dv.sem, t_dv[gi - 4])
                                    g.wait_ge(pe.sem, t_pe[gi - 4])
                                ins = g.dma_start(
                                    wld[gi % 4][:], w_in[wn].rearrange(
                                        "(b p) d -> b p d", p=128)[c2])
                                sem_of[gi].inc(ins, 16)
                                assert sem_of[gi].v == t_dG[gi]
                            if gi - 2 >= store_first:
                                emit_ffn_store(g, gi - 2)
                    for gi in (7 * NB - 2, 7 * NB - 1):
                        emit_ffn_store(g, gi)

                @blk.sync
                def _(sy):
                    sy.wait_ge(dS.sem, dS.v)
                    dS.inc(sy.dma_start(rowA[:], b_in["bo"][0:1, 0:512]), 16)
                    dS.inc(sy.dma_start(rowB[:], b_in["bo"][0:1, 512:1024]), 16)
                    bias_dS["bo"] = dS.v
                    rows_after = {7: ("wq", dv0 + 2), 15: ("wk", t_dv[7]),
                                  23: ("wv", t_dv[15])}
                    for wi, wn in enumerate(wplan):
                        for c2 in range(NB):
                            gi = wi * NB + c2
                            if gi % 2 == 1:
                                if gi >= 4:
                                    sy.wait_ge(dv.sem, t_dv[gi - 4])
                                    sy.wait_ge(pe.sem, t_pe[gi - 4])
                                ins = sy.dma_start(
                                    wld[gi % 4][:], w_in[wn].rearrange(
                                        "(b p) d -> b p d", p=128)[c2])
                                sem_of[gi].inc(ins, 16)
                                assert sem_of[gi].v == t_dG[gi]
                            if gi in rows_after:
                                rwn, rgate = rows_after[gi]
                                sy.wait_ge(dv.sem, rgate)
                                dS.inc(sy.dma_start(rowA[:],
                                                    brow_d[rwn][0:1, 0:512]), 16)
                                dS.inc(sy.dma_start(rowB[:],
                                                    brow_d[rwn][0:1, 512:1024]), 16)
                                bias_dS[rwn] = dS.v

                @blk.tensor
                def _(t):
                    t.wait_ge(dS.sem, bias_dS["bo"] - 32)
                    for wi, wn in enumerate(wplan):
                        if wn not in ("wq", "wk", "wv"):
                            continue
                        for c2 in range(NB):
                            gi = wi * NB + c2
                            t.wait_ge(sem_of[gi].sem, t_dG[gi])
                            if c2 == 0 and wi > 0:
                                t.wait_ge(dv.sem, t_dv[wi * NB - 1])
                            t.matmul(be1[0:1, :], bet_sb[:, c2:c2 + 1],
                                     wld[gi % 4][:, 0:512],
                                     start=(c2 == 0), stop=(c2 == NB - 1))
                            mm = t.matmul(be2[0:1, :], bet_sb[:, c2:c2 + 1],
                                          wld[gi % 4][:, 512:1024],
                                          start=(c2 == 0), stop=(c2 == NB - 1))
                            pe.inc(mm, 2)

                @blk.vector
                def _(v):
                    v.wait_ge(dS.sem, bias_dS["bo"])
                    ch(v, dv, v.tensor_copy(boe[0:1, 0:512], rowA[:]))
                    ch(v, dv, v.tensor_copy(boe[0:1, 512:1024], rowB[:]))
                    for wi, wn in enumerate(wplan):
                        for c2 in range(NB):
                            gi = wi * NB + c2
                            v.wait_ge(sem_of[gi].sem, t_dG[gi])
                            if wn in ("wq", "wk", "wv"):
                                v.wait_ge(pe.sem, t_pe[gi])
                                gsrc = gamq_sb if wn == "wq" else gam_sb
                                ins = v.tensor_scalar(wsb[wn][:, c2, :], wld[gi % 4][:],
                                                      gsrc[:, c2:c2 + 1], None, AX.mult)
                            else:
                                if gi - 2 in t_st:
                                    v.wait_ge(dTs[gi % 2].sem, t_st[gi - 2])
                                if wn == "wvf":
                                    ins = v.tensor_scalar(wcs[gi % 2][:], wld[gi % 4][:],
                                                          0.5, None, AX.mult)
                                else:
                                    ins = v.tensor_copy(wcs[gi % 2][:], wld[gi % 4][:])
                            ch(v, dv, ins)
                            if wn in ("wq", "wk", "wv") and c2 == NB - 1:
                                scl = 0.125 if wn == "wq" else 1.0
                                v.wait_ge(dS.sem, bias_dS[wn])
                                ch(v, dv, v.tensor_tensor(rowD[0:1, :], be1[0:1, :],
                                                          rowA[0:1, :], AX.add))
                                ch(v, dv, v.tensor_scalar(bds_[wn][0:1, 0:512],
                                                          rowD[0:1, :], scl, None, AX.mult))
                                ch(v, dv, v.tensor_tensor(rowD[0:1, :], be2[0:1, :],
                                                          rowB[0:1, :], AX.add))
                                ch(v, dv, v.tensor_scalar(bds_[wn][0:1, 512:1024],
                                                          rowD[0:1, :], scl, None, AX.mult))
                    assert dv.v == wA_dv, (dv.v, wA_dv)
                a0_pe_end = cp
                a0_dT_end = [dTs[0].v, dTs[1].v]

            with ExitStack() as psc:
                EP = psc.enter_context
                xg = [EP(nc.sbuf_tensor(f"xg{i}", [128, D], F32)) for i in range(4)]
                xn_tok = [EP(nc.sbuf_tensor(f"xntk{i}", [128, D], BF16)) for i in range(4)]
                xnt_g = EP(nc.sbuf_tensor("xnt_g", [128, NB, 512], BF16))
                kt_ev = [EP(nc.sbuf_tensor(f"ktev{i}", [128, 512], BF16)) for i in range(2)]
                v_ev = [EP(nc.sbuf_tensor(f"vev{i}", [128, NB, 132], BF16)) for i in range(2)]
                sumA = EP(nc.sbuf_tensor("sumA", [128, NT], F32))
                sqA = EP(nc.sbuf_tensor("sqA", [128, NT], F32))
                stm = EP(nc.sbuf_tensor("stm", [128, NT], F32))
                stv = EP(nc.sbuf_tensor("stv", [128, NT], F32))
                stn = EP(nc.sbuf_tensor("stn", [128, NT], F32))
                tr_ps = [EP(nc.psum_tensor(f"trp{i}", [128, 1024], BF16)) for i in range(2)]

                @blk.vector
                def _(v):
                    v.wait_ge(dv.sem, wA_dv)
                    v.wait_ge(pe.sem, a0_pe_end)
                    v.wait_ge(dTs[0].sem, a0_dT_end[0])
                    v.wait_ge(dTs[1].sem, a0_dT_end[1])
                    for i2 in range(2):
                        ch(v, dv, v.memset(v_ev[i2][:, :, 64:66], 1.0))
                        ch(v, dv, v.memset(v_ev[i2][:, :, 130:132], 1.0))
                a2_handover_dv = dv.v
                q1_scope = ExitStack()
                q1_ps = q1_scope.enter_context(nc.psum_tensor("q1p", [128, 8], F32))

                xr = xkv_d.rearrange("(n p) d -> n p d", p=128)

                @blk.sync
                def _(sy):
                    sy.wait_ge(dS.sem, dS.v)
                    sy.wait_ge(dv.sem, a2_handover_dv)
                    dS.inc(sy.dma_start(xg[0][0:1, :], xb_d), 16)

                @blk.scalar
                def _(s):
                    s.wait_ge(dS.sem, dS.v)
                    s.wait_ge(dv.sem, a2_handover_dv)
                    ch(s, ac, s.activation(xn_tok[0][0:1, :], xg[0][0:1, :], AF.Identity,
                                           accum_out=sumA[0:1, 0:1]))
                    ch(s, ac, s.activation(xn_tok[0][0:1, :], xg[0][0:1, :], AF.Square,
                                           accum_out=sqA[0:1, 0:1]))

                @blk.vector
                def _(v):
                    v.wait_ge(ac.sem, ac.v)
                    ch(v, dv, v.tensor_scalar(stm[0:1, 0:1], sumA[0:1, 0:1], 1.0 / D, None, AX.mult))
                    ch(v, dv, v.tensor_scalar(stv[0:1, 0:1], sqA[0:1, 0:1], 1.0 / D, None, AX.mult))
                    ch(v, dv, v.tensor_tensor(stn[0:1, 0:1], stm[0:1, 0:1], stm[0:1, 0:1], AX.mult))
                    ch(v, dv, v.tensor_sub(stv[0:1, 0:1], stv[0:1, 0:1], stn[0:1, 0:1]))

                @blk.scalar
                def _(s):
                    s.wait_ge(dv.sem, dv.v)
                    ch(s, ac, s.activation(stv[0:1, 0:1], stv[0:1, 0:1], AF.Ln, bias=epsA[0:1, :]))
                    ch(s, ac, s.activation(stv[0:1, 0:1], stv[0:1, 0:1], AF.Exp, scale=-0.5))

                @blk.vector
                def _(v):
                    v.wait_ge(ac.sem, ac.v)
                    ch(v, dv, v.scalar_tensor_tensor(stn[0:1, 0:1], stm[0:1, 0:1], -1.0,
                                                     stv[0:1, 0:1], op0=AX.mult, op1=AX.mult))
                    ch(v, dv, v.tensor_scalar(xbn_sb[0:1, :], xg[0][0:1, :],
                                              stv[0:1, 0:1], stn[0:1, 0:1], AX.mult, AX.add))

                @blk.tensor
                def _(t):
                    t.wait_ge(dv.sem, dv.v)
                    for b2 in range(NB):
                        mm = t.matmul(q1_ps[:, b2:b2 + 1],
                                      xbn_sb[0:1, 128 * b2:128 * (b2 + 1)],
                                      ones_r[0:1, 0:1], start=True, stop=True)
                    pe.inc(mm)

                @blk.vector
                def _(v):
                    v.wait_ge(pe.sem, pe.v)
                    ch(v, dv, v.tensor_copy(xbt_sb[:], q1_ps[:]))

                @blk.tensor
                def _(t):
                    t.wait_ge(dv.sem, dv.v)
                    for d2 in range(NB):
                        for c2 in range(NB):
                            t.matmul(q1_ps[:, d2:d2 + 1],
                                     wq_sb[:, c2, 128 * d2:128 * (d2 + 1)],
                                     xbt_sb[:, c2:c2 + 1], start=(c2 == 0), stop=False)
                        mm = t.matmul(q1_ps[:, d2:d2 + 1],
                                      bqe[0:1, 128 * d2:128 * (d2 + 1)],
                                      ones_r[0:1, 0:1], start=False, stop=True)
                    pe.inc(mm)

                @blk.vector
                def _(v):
                    v.wait_ge(pe.sem, pe.v)
                    ch(v, dv, v.tensor_copy(qt_sb[:, :, 0:1],
                                            q1_ps[:].rearrange("p (a o) -> p a o", o=1)))

                q1_scope.close()
                pj_ps = [EP(nc.psum_tensor(f"pjp{i}", [128, 512], F32)) for i in range(3)]
                kt_ps = [EP(nc.psum_tensor(f"ktp{i}", [128, 512], F32)) for i in range(3)]
                vN, vE, vVE, pR, pV, aS, aR, dX, gV = {}, {}, {}, {}, {}, {}, {}, {}, {}
                vVEh, vKE, vQE, pK, pQ, gK = {}, {}, {}, {}, {}, {}
                last_tr_evac = {}
                last_pj_evac = {}
                last_kt_bank = {}
                ktq_ctr = [0]

                def emit_sync(t):
                    @blk.sync
                    def _(sy, t=t):
                        if t < 4:
                            sy.wait_ge(dv.sem, a2_handover_dv)
                            sy.wait_ge(dv.sem, sweep0_dv)
                        if t >= 4:
                            sy.wait_ge(dv.sem, vNv[t - 4])
                        dXs[t % 4].inc(sy.dma_start(xg[t % 4][:], xr[t]), 16)
                        dX[t] = dXs[t % 4].v

                def emit_norm(t):
                    @blk.vector
                    def _(v, t=t):
                        ins = v.tensor_scalar(xn_tok[t % 4][:], xg[t % 4][:],
                                              stv[:, t:t + 1], stn[:, t:t + 1],
                                              AX.mult, AX.add)
                        dv.inc(ins)
                        vN[t] = dv.v
                        assert dv.v == vNv[t]

                def emit_transpose(t):
                    @blk.tensor
                    def _(tn, t=t):
                        tn.wait_ge(dv.sem, vN[t])
                        if t >= 2:
                            tn.wait_ge(ac.sem, vEa[t - 2])
                        for b2 in range(NB):
                            mm = tn.transpose(tr_ps[t % 2][:, 128 * b2:128 * (b2 + 1)],
                                              xn_tok[t % 4][:, 128 * b2:128 * (b2 + 1)],
                                              idb[:])
                        pe.inc(mm)
                        pR[t] = pe.v
                        assert pe.v == pRv[t]

                def emit_vproj(t):
                    @blk.tensor
                    def _(tn, t=t):
                        tn.wait_ge(ac.sem, vEa[t])
                        for hf in range(2):
                            bk = (2 * t + hf) % 3
                            if bk in last_pj_evac:
                                tn.wait_ge(dv.sem, last_pj_evac[bk])
                            for c2 in range(NB):
                                tn.matmul(pj_ps[bk][:],
                                          xnt_g[:, c2, 128 * (t % 4):128 * (t % 4 + 1)],
                                          wv_sb[:, c2, 512 * hf:512 * (hf + 1)],
                                          start=(c2 == 0), stop=False)
                            mm = tn.matmul(pj_ps[bk][:], ones_r[0:1, 0:128],
                                           bve[0:1, 512 * hf:512 * (hf + 1)],
                                           start=False, stop=True)
                            pe.inc(mm)
                        pV[t] = pe.v

                def emit_vevac(t):
                    @blk.vector
                    def _(v, t=t):
                        for hf in range(2):
                            bk = (2 * t + hf) % 3
                            v.wait_ge(pe.sem, pV[t] - (1 - hf))
                            if t >= 2:
                                v.wait_ge(dVs[t % 2].sem, gV[t - 2])
                            dst = v_ev[t % 2][:, 4 * hf:4 * (hf + 1), :].rearrange(
                                "p a (h x) -> p a h x", h=2)[:, :, :, 0:64]
                            src = pj_ps[bk][:].rearrange("p (a h x) -> p a h x",
                                                         a=4, h=2)
                            ins = v.tensor_copy(dst, src)
                            dv.inc(ins)
                            last_pj_evac[bk] = dv.v
                            v.wait_ge(dv.sem, dv.v)
                        vVE[t] = dv.v

                def emit_vstore(t):
                    @blk.gpsimd
                    def _(g, t=t):
                        g.wait_ge(dv.sem, vVE[t])
                        for p2 in range(NB):
                            ins = g.dma_start(
                                vd[p2, t // 4, :, 132 * (t % 4):132 * (t % 4) + 132],
                                v_ev[t % 2][:, p2, :])
                            dVs[t % 2].inc(ins, 16)
                        gV[t] = dVs[t % 2].v

                def emit_ktproj(hh):
                    for d2 in range(NB):
                        ci2 = ktq_ctr[0]
                        ktq_ctr[0] += 1
                        bk = bk_of[("kt", hh, d2)]
                        assert bk == ci2 % 3

                        @blk.tensor
                        def _(tn, hh=hh, d2=d2, bk=bk, ci2=ci2):
                            tn.wait_ge(ac.sem, vEa[4 * hh + 3])
                            if ci2 >= 3:
                                tn.wait_ge(ac.sem, evA[ci2 - 3])
                            for c2 in range(NB):
                                tn.matmul(kt_ps[bk][:],
                                          wk_sb[:, c2, 128 * d2:128 * (d2 + 1)],
                                          xnt_g[:, c2, 0:512],
                                          start=(c2 == 0), stop=False)
                            mm = tn.matmul(kt_ps[bk][:],
                                           bke[0:1, 128 * d2:128 * (d2 + 1)],
                                           ones_r[0:1, 0:512], start=False, stop=True)
                            pe.inc(mm)
                            assert pe.v == pKv[(hh, d2)]

                        @blk.gpsimd
                        def _(g, hh=hh, d2=d2):
                            g.wait_ge(ac.sem, kEa[(hh, d2)])
                            ins = g.dma_start(ktd[128 * d2:128 * (d2 + 1),
                                                  512 * hh:512 * (hh + 1)],
                                              kt_ev[d2 % 2][:])
                            dKs[d2 % 2].inc(ins, 16)
                            assert dKs[d2 % 2].v == gKv[(hh, d2)]

                def emit_qproj(hh):
                    qw = min(512, TOKQ - 512 * hh)
                    for d2 in range(NB):
                        ci2 = ktq_ctr[0]
                        ktq_ctr[0] += 1
                        bk = bk_of[("q", hh, d2)]
                        assert bk == ci2 % 3

                        @blk.tensor
                        def _(tn, hh=hh, d2=d2, qw=qw, bk=bk, ci2=ci2):
                            tn.wait_ge(ac.sem, vEa[4 * hh + 3])
                            if ci2 >= 3:
                                tn.wait_ge(ac.sem, evA[ci2 - 3])
                            for c2 in range(NB):
                                tn.matmul(kt_ps[bk][:, 0:qw],
                                          wq_sb[:, c2, 128 * d2:128 * (d2 + 1)],
                                          xnt_g[:, c2, 0:qw],
                                          start=(c2 == 0), stop=False)
                            mm = tn.matmul(kt_ps[bk][:, 0:qw],
                                           bqe[0:1, 128 * d2:128 * (d2 + 1)],
                                           ones_r[0:1, 0:qw], start=False, stop=True)
                            pe.inc(mm)
                            assert pe.v == pQv[(hh, d2)]

                sweep0_dv = dv.v
                acB, dvB2, peB2 = ac.v, dv.v, pe.v
                dxB = [dXs[i].v for i in range(4)]
                NG4 = NT // 4
                dXv, aS2, aRg, chgv, pRv, vNv = {}, {}, {}, {}, {}, {}
                cvx, cpx = dvB2, peB2
                pVv, pKv, pQv, vEa = {}, {}, {}, {}
                kEa, qEa, bk_of, gKv = {}, {}, {}, {}
                ktq_seq = []
                ctr2 = 0
                for g in range(NG4):
                    for d2 in range(NB):
                        bk_of[("kt", g, d2)] = ctr2 % 3
                        ktq_seq.append(("kt", g, d2))
                        ctr2 += 1
                    if g in QHALVES:
                        for d2 in range(NB):
                            bk_of[("q", g, d2)] = ctr2 % 3
                            ktq_seq.append(("q", g, d2))
                            ctr2 += 1
                cdk = [dKs[0].v, dKs[1].v]
                for g in range(NG4):
                    for d2 in range(NB):
                        cdk[d2 % 2] += 16
                        gKv[(g, d2)] = cdk[d2 % 2]
                for g in range(NG4):
                    chgv[g] = cvx + 4
                    cvx += 5
                    for t in range(4 * g, 4 * g + 4):
                        cvx += 1
                        vNv[t] = cvx
                        cpx += 1
                        pRv[t] = cpx
                    for t in range(4 * g, 4 * g + 4):
                        cpx += 2
                        pVv[t] = cpx
                        cvx += 2
                    hh = g
                    for d2 in range(NB):
                        cpx += 1
                        pKv[(hh, d2)] = cpx
                    if hh in QHALVES:
                        for d2 in range(NB):
                            cpx += 1
                            pQv[(hh, d2)] = cpx
                cax = acB
                for g in range(NG4):
                    for t in range(4 * g, 4 * g + 4):
                        dXv[t] = dxB[t % 4] + 16 * (t // 4 + 1)
                        cax += 2
                        aS2[t] = cax
                    cax += 2
                    aRg[g] = cax
                    if g >= 1:
                        hp2 = g - 1
                        for d2 in range(NB):
                            cax += 1
                            kEa[(hp2, d2)] = cax
                        if hp2 in QHALVES:
                            for d2 in range(NB):
                                cax += 1
                                qEa[(hp2, d2)] = cax
                    for t in range(4 * g, 4 * g + 4):
                        cax += 1
                        vEa[t] = cax
                hp2 = NG4 - 1
                for d2 in range(NB):
                    cax += 1
                    kEa[(hp2, d2)] = cax
                if hp2 in QHALVES:
                    for d2 in range(NB):
                        cax += 1
                        qEa[(hp2, d2)] = cax
                evA = [(kEa if kk == "kt" else qEa)[(hh_, d2_)]
                       for (kk, hh_, d2_) in ktq_seq]

                @blk.scalar
                def _(s):
                    for g in range(NG4):
                        for t in range(4 * g, 4 * g + 4):
                            s.wait_ge(dXs[t % 4].sem, dXv[t])
                            if t == 0:
                                s.wait_ge(dv.sem, sweep0_dv)
                            if t >= 4:
                                s.wait_ge(pe.sem, pRv[t - 4])
                            ch(s, ac, s.activation(xn_tok[t % 4][:], xg[t % 4][:], AF.Identity,
                                                   accum_out=sumA[:, t:t + 1]))
                            ch(s, ac, s.activation(xn_tok[t % 4][:], xg[t % 4][:], AF.Square,
                                                   accum_out=sqA[:, t:t + 1]))
                            assert ac.v == aS2[t]
                        s.wait_ge(dv.sem, chgv[g])
                        ch(s, ac, s.activation(stv[:, 4 * g:4 * g + 4], stv[:, 4 * g:4 * g + 4],
                                               AF.Ln, bias=epsA[:]))
                        ch(s, ac, s.activation(stv[:, 4 * g:4 * g + 4], stv[:, 4 * g:4 * g + 4],
                                               AF.Exp, scale=-0.5))
                        assert ac.v == aRg[g]
                        if g >= 1:
                            emit_ktq_evacs(s, g - 1)
                        for t in range(4 * g, 4 * g + 4):
                            s.wait_ge(pe.sem, pRv[t])
                            if t >= 4:
                                t4 = t - 4
                                s.wait_ge(pe.sem, pVv[t4])
                                hh4 = t4 // 4
                                if (hh4, NB - 1) in pKv:
                                    s.wait_ge(pe.sem, pKv[(hh4, NB - 1)])
                                if (hh4, NB - 1) in pQv:
                                    s.wait_ge(pe.sem, pQv[(hh4, NB - 1)])
                            ch(s, ac, s.activation(
                                xnt_g[:, :, 128 * (t % 4):128 * (t % 4 + 1)],
                                tr_ps[t % 2][:].rearrange("p (b w) -> p b w", b=NB),
                                AF.Identity))
                            assert ac.v == vEa[t]
                        def emit_ktq_evacs(s, hp2):
                            for d2 in range(NB):
                                s.wait_ge(pe.sem, pKv[(hp2, d2)])
                                if d2 >= 2:
                                    s.wait_ge(dKs[d2 % 2].sem, gKv[(hp2, d2 - 2)])
                                elif hp2 >= 1:
                                    s.wait_ge(dKs[d2 % 2].sem,
                                              gKv[(hp2 - 1, d2 + NB - 2)])
                                ch(s, ac, s.activation(
                                    kt_ev[d2 % 2][:],
                                    kt_ps[bk_of[("kt", hp2, d2)]][:], AF.Identity))
                                assert ac.v == kEa[(hp2, d2)]
                            if hp2 in QHALVES:
                                lq2 = 512 * hp2 + 1
                                qw2 = min(512, TOKQ - 512 * hp2)
                                for d2 in range(NB):
                                    s.wait_ge(pe.sem, pQv[(hp2, d2)])
                                    ch(s, ac, s.activation(
                                        qt_sb[:, d2, lq2:lq2 + qw2],
                                        kt_ps[bk_of[("q", hp2, d2)]][:, 0:qw2],
                                        AF.Identity))
                                    assert ac.v == qEa[(hp2, d2)]
                    emit_ktq_evacs(s, NG4 - 1)

                def emit_groupchain(g):
                    @blk.vector
                    def _(v, g=g):
                        sl = slice(4 * g, 4 * g + 4)
                        v.wait_ge(ac.sem, aS2[4 * g + 3])
                        ch(v, dv, v.tensor_scalar(stm[:, sl], sumA[:, sl], 1.0 / D, None, AX.mult))
                        ch(v, dv, v.tensor_scalar(stv[:, sl], sqA[:, sl], 1.0 / D, None, AX.mult))
                        ch(v, dv, v.tensor_tensor(stn[:, sl], stm[:, sl], stm[:, sl], AX.mult))
                        ch(v, dv, v.tensor_sub(stv[:, sl], stv[:, sl], stn[:, sl]))
                        assert dv.v == chgv[g]
                        v.wait_ge(ac.sem, aRg[g])
                        ch(v, dv, v.scalar_tensor_tensor(stn[:, sl], stm[:, sl], -1.0,
                                                         stv[:, sl], op0=AX.mult, op1=AX.mult))

                for g in range(NT // 4):
                    for t in range(4 * g, 4 * g + 4):
                        emit_sync(t)
                    emit_groupchain(g)
                    for t in range(4 * g, 4 * g + 4):
                        emit_norm(t)
                        emit_transpose(t)
                    for tt2 in range(4 * g, 4 * g + 4):
                        emit_vproj(tt2)
                        emit_vevac(tt2)
                        emit_vstore(tt2)
                    emit_ktproj(g)
                    if g in QHALVES:
                        emit_qproj(g)

                a_end_dv, a_end_pe, a_end_ac = dv.v, pe.v, ac.v
                a_end_dV = [dVs[0].v, dVs[1].v]
                a_end_dK = [dKs[0].v, dKs[1].v]
                a_end_dT = [dTs[0].v, dTs[1].v]
                a_end_dW = [dWs[0].v, dWs[1].v]
                a_end_dX = [dXs[i].v for i in range(4)]

        with ExitStack() as bscope:
            EB = bscope.enter_context
            ktb = [EB(nc.sbuf_tensor(f"ktb{i}", [128, 512], BF16)) for i in range(4)]
            vb = [EB(nc.sbuf_tensor(f"vbx{i}", [128, 528], BF16)) for i in range(4)]
            tbuf = [EB(nc.sbuf_tensor(f"tbf{i}", [128, 1024], BF16)) for i in range(WBUF)]
            ebuf = [EB(nc.sbuf_tensor(f"ebf{i}", [128, 1024], BF16)) for i in range(WBUF)]
            wbuf = [EB(nc.sbuf_tensor(f"wbf{i}", [128, 1024], BF16)) for i in range(WBUF)]
            xq2 = [EB(nc.sbuf_tensor(f"xq2{i}", [128, D], F32)) for i in range(2)]
            xt_sb = EB(nc.sbuf_tensor("xt_sb", [128, NB, 512], F32))
            sq_sb = EB(nc.sbuf_tensor("sq_sb", [128, 2, 512], F32))
            att_sb = EB(nc.sbuf_tensor("att_sb", [128, NB, 512], BF16))
            dinv = EB(nc.sbuf_tensor("dinv", [1, 1024], F32))
            dinvb = EB(nc.sbuf_tensor("dinvb", [64, 1024], F32))
            woh_sb = EB(nc.sbuf_tensor("woh_sb", [128, NB, 512], BF16))

            wo_prev_pe = [None]
            pend_epi = [None]
            prev_epi_done = [None]
            gpair_ctr = [0]
            xt_ready = [None]
            prev_xq_pe = [None]

            def emit_xt_prep(nqi, gate_dv, pb, sync_gate=()):
                Wn, qon = WL[nqi], QOFF[nqi]
                ntbn = (Wn + 127) // 128
                xdma_, tpv_, tev_ = {}, {}, {}
                for tbk in range(ntbn):
                    r = min(Wn - 128 * tbk, 128)

                    @blk.sync
                    def _(sy, tbk=tbk, pxq=prev_xq_pe[0], sg=tuple(sync_gate)):
                        if tbk == 0:
                            for sem_, val in sg:
                                sy.wait_ge(sem_, val)
                        if tbk == 0 and pxq is not None:
                            sy.wait_ge(pe.sem, pxq)
                        if tbk >= 2:
                            sy.wait_ge(pe.sem, tpv_[tbk - 2])
                        r0 = qon + 128 * tbk
                        r1 = min(qon + Wn, r0 + 128)
                        ss = dQs[tbk % 2]
                        if r0 == 0:
                            ss.inc(sy.dma_start(xq2[tbk % 2][0:1, :], xb_d), 16)
                            if r1 > 1:
                                ss.inc(sy.dma_start(xq2[tbk % 2][1:r1, :],
                                                    xkv_d[0:r1 - 1, :]), 16)
                        else:
                            ss.inc(sy.dma_start(xq2[tbk % 2][0:r1 - r0, :],
                                                xkv_d[r0 - 1:r1 - 1, :]), 16)
                        xdma_[tbk] = ss.v

                    @blk.tensor
                    def _(tn, tbk=tbk, r=r, gate_dv=gate_dv):
                        tn.wait_ge(dQs[tbk % 2].sem, xdma_[tbk])
                        if tbk < 2:
                            tn.wait_ge(dv.sem, gate_dv)
                        else:
                            tn.wait_ge(dv.sem, tev_[tbk - 2])
                        bA = pb[2 * (tbk % 2)]
                        bB = pb[2 * (tbk % 2) + 1]
                        for b2 in range(NB):
                            dst = bA if b2 < 4 else bB
                            cb = 128 * (b2 % 4)
                            mm = tn.transpose(dst[:, cb:cb + r],
                                              xq2[tbk % 2][0:r, 128 * b2:128 * (b2 + 1)],
                                              idf[0:r, 0:r])
                        pe.inc(mm)
                        tpv_[tbk] = pe.v

                    @blk.vector
                    def _(v, tbk=tbk, r=r):
                        v.wait_ge(pe.sem, tpv_[tbk])
                        bA = pb[2 * (tbk % 2)]
                        bB = pb[2 * (tbk % 2) + 1]
                        ch(v, dv, v.tensor_copy(
                            xt_sb[:, 0:4, 128 * tbk:128 * tbk + r],
                            bA[:].rearrange("p (b w) -> p b w", b=4)[:, :, 0:r]))
                        ch(v, dv, v.tensor_copy(
                            xt_sb[:, 4:8, 128 * tbk:128 * tbk + r],
                            bB[:].rearrange("p (b w) -> p b w", b=4)[:, :, 0:r]))
                    tev_[tbk] = dv.v
                prev_xq_pe[0] = tpv_[ntbn - 1]
                xt_ready[0] = tev_[ntbn - 1]

            with ExitStack() as prosc:
                xtp = [prosc.enter_context(nc.psum_tensor(f"xtp{i}", [128, 512], F32))
                       for i in range(4)]
                emit_xt_prep(0, a_end_dv, xtp,
                             sync_gate=[(pe.sem, a_end_pe), (dv.sem, a_end_dv),
                                        (ac.sem, a_end_ac)])

            @blk.tensor
            def _(tn, g0=xt_ready[0]):
                tn.wait_ge(dv.sem, g0)

            for qi, W in enumerate(WL):
                qo = QOFF[qi]
                with ExitStack() as qsc:
                    EQ = qsc.enter_context
                    st_ps = EQ(nc.psum_tensor(f"st_ps{qi}", [128, 2048], F32))
                    o_ps = [[EQ(nc.psum_tensor(f"o{h}_ps{qi}_{par}", [128, 512], F32))
                             for h in range(2)] for par in range(2)]

                    qm1B, qs1B = m1.v, s1.v

                    @blk.scalar
                    def _(s, W=W, qm1B=qm1B, qs1B=qs1B):
                        for p3 in range(NB):
                            m1B_, s1B_ = qm1B + KS * p3, qs1B + KS * p3
                            for k in range(KS):
                                s.wait_ge(m1.sem, m1B_ + k + 1)
                                s.wait_ge(s1.sem, max(0, s1B_ + k - WBUF + 1))
                                sgl = 1024 * (k % 2)
                                srcv = st_ps[:, sgl:sgl + 1024].rearrange(
                                    "p (h w) -> p h w", h=2)[:, :, 0:W]
                                ac.inc(s.activation(
                                    tbuf[k % WBUF][:].rearrange("p (h w) -> p h w", h=2)[:, :, 0:W],
                                    srcv, AF.Tanh, bias=mubc[:], scale=0.5))
                                a2.inc(s.activation(
                                    ebuf[k % WBUF][:].rearrange("p (h w) -> p h w", h=2)[:, :, 0:W],
                                    srcv, AF.Exp, bias=ln2b[:], scale=1.0))

                    qa2B = a2.v - NB * KS
                    for p2 in range(NB):
                        m1B, s1B, m2B = m1.v, s1.v, m2.v
                        a2B = qa2B + KS * p2
                        first = (qi == 0 and p2 == 0)
                        gpair = gpair_ctr[0]
                        kvk, kvv = {}, {}
                        epi = pend_epi[0]
                        dvB = dv.v
                        bcpe = {}

                        @blk.gpsimd
                        def _(g, p2=p2, m1B=m1B, first=first):
                            if first:
                                g.wait_ge(dv.sem, a_end_dv)
                                g.wait_ge(pe.sem, a_end_pe)
                                g.wait_ge(dKs[0].sem, a_end_dK[0])
                                g.wait_ge(dKs[1].sem, a_end_dK[1])
                                g.wait_ge(dWs[0].sem, a_end_dW[0])
                                g.wait_ge(dWs[1].sem, a_end_dW[1])
                                g.wait_ge(dTs[0].sem, a_end_dT[0])
                                g.wait_ge(dTs[1].sem, a_end_dT[1])
                            kdist = 4 if NCH >= 4 else NCH
                            for c2 in range(NCH):
                                g.wait_ge(m1.sem, max(0, m1B + 4 * c2 - 4 * kdist + 4))
                                ksem = (dWs + dKs)[c2 % 4]
                                ins = g.dma_start(
                                    ktb[c2 % 4][:],
                                    ktd.rearrange("(p r) s -> p r s", r=128)[p2][
                                        :, 512 * c2:512 * (c2 + 1)])
                                ksem.inc(ins, 16)
                                kvk[c2] = ksem.v

                        @blk.sync
                        def _(sy, p2=p2, m2B=m2B, first=first):
                            if first:
                                sy.wait_ge(dv.sem, a_end_dv)
                                sy.wait_ge(pe.sem, a_end_pe)
                                sy.wait_ge(dVs[0].sem, a_end_dV[0])
                                sy.wait_ge(dVs[1].sem, a_end_dV[1])
                                for i4 in range(4):
                                    sy.wait_ge(dXs[i4].sem, a_end_dX[i4])
                            kdist = 4 if NCH >= 4 else NCH
                            for c2 in range(NCH):
                                sy.wait_ge(m2.sem, max(0, m2B + 4 * c2 - 4 * kdist + 4))
                                dXs[c2 % 4].inc(sy.dma_start(vb[c2 % 4][:], vd[p2, c2]), 16)
                                kvv[c2] = dXs[c2 % 4].v

                        @blk.tensor
                        def _(tn, p2=p2, W=W, qo=qo, kvk=kvk, kvv=kvv, a2B=a2B,
                              s1B=s1B, first=first, gpair=gpair, epi=epi, dvB=dvB):
                            if first:
                                tn.wait_ge(dv.sem, a_end_dv)
                                tn.wait_ge(dTs[0].sem, a_end_dT[0])
                                tn.wait_ge(dTs[1].sem, a_end_dT[1])
                            par = gpair % 2
                            for k in range(KS + 2):
                                if k < KS:
                                    c2, j = k // 4, k % 4
                                    tn.wait_ge((dWs + dKs)[c2 % 4].sem, kvk[c2])
                                    tn.wait_ge(a2.sem, max(0, a2B + k - 1))
                                    sgl = 1024 * (k % 2)
                                    tn.matmul(st_ps[:, sgl:sgl + W],
                                              ktb[c2 % 4][0:64, 128 * j:128 * (j + 1)],
                                              qt_sb[0:64, p2, qo:qo + W],
                                              start=True, stop=True, tile_position=(0, 0))
                                    mm = tn.matmul(st_ps[:, sgl + 512:sgl + 512 + W],
                                                   ktb[c2 % 4][64:128, 128 * j:128 * (j + 1)],
                                                   qt_sb[64:128, p2, qo:qo + W],
                                                   start=True, stop=True, tile_position=(64, 0))
                                    m1.inc(mm)
                                if epi is not None and k == 3:
                                    opv, Wp = epi[0], epi[1]
                                    tn.wait_ge(dv.sem, dvB + 1)
                                    mmb = tn.matmul(opv[0][64:128, 0:Wp], onesf_r[0:1, 0:64],
                                                    dinv[0:1, 0:Wp], start=True, stop=True,
                                                    tile_position=(0, 64))
                                    bcpe[0] = pe.inc(mmb)
                                if epi is not None and k == 5:
                                    opv, Wp = epi[0], epi[1]
                                    tn.wait_ge(dv.sem, dvB + 2)
                                    mmb = tn.matmul(opv[1][64:128, 0:Wp], onesf_r[0:1, 0:64],
                                                    dinv[0:1, 512:512 + Wp], start=True, stop=True,
                                                    tile_position=(0, 64))
                                    bcpe[1] = pe.inc(mmb)
                                k2 = k - 2
                                if k2 >= 0:
                                    c2, j = k2 // 4, k2 % 4
                                    tn.wait_ge(s1.sem, s1B + k2 + 1)
                                    tn.wait_ge(dXs[c2 % 4].sem, kvv[c2])
                                    if k2 == 0:
                                        tn.wait_ge(at.sem, max(0, gpair - 1))
                                        if p2 == 0 and prev_epi_done[0] is not None:
                                            tn.wait_ge(dv.sem, prev_epi_done[0])
                                    tn.matmul(o_ps[par][0][0:65, 0:W],
                                              vb[c2 % 4][:, 132 * j:132 * j + 65],
                                              wbuf[k2 % WBUF][:, 0:W],
                                              start=(k2 == 0), stop=(k2 == KS - 1))
                                    mm = tn.matmul(o_ps[par][1][0:65, 0:W],
                                                   vb[c2 % 4][:, 132 * j + 66:132 * j + 131],
                                                   wbuf[k2 % WBUF][:, 512:512 + W],
                                                   start=(k2 == 0), stop=(k2 == KS - 1))
                                    m2.inc(mm)

                        @blk.vector
                        def _(v, W=W, a2B=a2B, m2B=m2B, epi=epi, dvB=dvB, gpair=gpair):
                            for k in range(KS):
                                v.wait_ge(a2.sem, a2B + k + 1)
                                v.wait_ge(m2.sem, max(0, m2B + k - WBUF + 1))
                                vw = lambda buf: buf.rearrange("p (h w) -> p h w", h=2)[:, :, 0:W]
                                s1.inc(v.scalar_tensor_tensor(
                                    vw(wbuf[k % WBUF][:]), vw(tbuf[k % WBUF][:]), 1.0,
                                    vw(ebuf[k % WBUF][:]), op0=AX.add, op1=AX.mult))
                                if epi is not None and k < 8:
                                    opv, Wp, pprev = epi[0], epi[1], epi[2]
                                    if k == 0:
                                        v.wait_ge(m2.sem, m2B)
                                        assert dv.v == dvB
                                        dv.inc(v.tensor_copy(dinv[0:1, 0:Wp],
                                                             opv[0][64:65, 0:Wp]))
                                        v.wait_ge(dv.sem, dv.v)
                                    elif k == 1:
                                        dv.inc(v.tensor_copy(dinv[0:1, 512:512 + Wp],
                                                             opv[1][64:65, 0:Wp]))
                                        v.wait_ge(dv.sem, dv.v)
                                    elif k == 4:
                                        v.wait_ge(pe.sem, bcpe[0])
                                        dv.inc(v.reciprocal(dinvb[:, 0:Wp],
                                                            opv[0][64:128, 0:Wp]))
                                        v.wait_ge(dv.sem, dv.v)
                                    elif k == 5:
                                        dv.inc(v.tensor_tensor(att_sb[0:64, pprev, 0:Wp],
                                                               opv[0][0:64, 0:Wp],
                                                               dinvb[:, 0:Wp], AX.mult))
                                    elif k == 6:
                                        v.wait_ge(pe.sem, bcpe[1])
                                        dv.inc(v.reciprocal(dinvb[:, 512:512 + Wp],
                                                            opv[1][64:128, 0:Wp]))
                                        v.wait_ge(dv.sem, dv.v)
                                    elif k == 7:
                                        ins = v.tensor_tensor(att_sb[64:128, pprev, 0:Wp],
                                                              opv[1][0:64, 0:Wp],
                                                              dinvb[:, 512:512 + Wp], AX.mult)
                                        at.inc(ins)

                        pend_epi[0] = (o_ps[gpair % 2], W, p2)
                        gpair_ctr[0] += 1

                    opv, Wp, pprev = pend_epi[0]
                    pend_epi[0] = None
                    tm2 = m2.v

                    @blk.vector
                    def _(v, opv=opv, Wp=Wp):
                        v.wait_ge(m2.sem, tm2)
                        ch(v, dv, v.tensor_copy(dinv[0:1, 0:Wp], opv[0][64:65, 0:Wp]))
                        ch(v, dv, v.tensor_copy(dinv[0:1, 512:512 + Wp], opv[1][64:65, 0:Wp]))
                    te0 = dv.v

                    @blk.tensor
                    def _(tn, opv=opv, Wp=Wp, te0=te0):
                        tn.wait_ge(dv.sem, te0)
                        tn.matmul(opv[0][64:128, 0:Wp], onesf_r[0:1, 0:64],
                                  dinv[0:1, 0:Wp], start=True, stop=True,
                                  tile_position=(0, 64))
                        mmb = tn.matmul(opv[1][64:128, 0:Wp], onesf_r[0:1, 0:64],
                                        dinv[0:1, 512:512 + Wp], start=True, stop=True,
                                        tile_position=(0, 64))
                        pe.inc(mmb)
                    te1 = pe.v

                    @blk.vector
                    def _(v, opv=opv, Wp=Wp, pprev=pprev, te1=te1):
                        v.wait_ge(pe.sem, te1)
                        ch(v, dv, v.reciprocal(dinvb[:, 0:Wp], opv[0][64:128, 0:Wp]))
                        ch(v, dv, v.reciprocal(dinvb[:, 512:512 + Wp], opv[1][64:128, 0:Wp]))
                        ch(v, dv, v.tensor_tensor(att_sb[0:64, pprev, 0:Wp],
                                                  opv[0][0:64, 0:Wp], dinvb[:, 0:Wp], AX.mult))
                        ins = v.tensor_tensor(att_sb[64:128, pprev, 0:Wp],
                                              opv[1][0:64, 0:Wp], dinvb[:, 512:512 + Wp], AX.mult)
                        at.inc(ins)
                        v.wait_ge(at.sem, at.v)

                with ExitStack() as esc:
                    EE = esc.enter_context
                    pp_ps = [EE(nc.psum_tensor(f"pp{qi}_{i}", [128, 512], F32)) for i in range(4)]
                    sm_ps = EE(nc.psum_tensor(f"sm_ps{qi}", [1, 512], F32))
                    sq_ps = EE(nc.psum_tensor(f"sq_ps{qi}", [1, 512], F32))
                    rb_ps = EE(nc.psum_tensor(f"rb_ps{qi}", [128, 512], F32))
                    nb_ps = EE(nc.psum_tensor(f"nb_ps{qi}", [128, 512], F32))

                    ntb = (W + 127) // 128
                    atE = at.v
                    xt_rd = xt_ready[0]

                    ppv, ypv, sqv2, stv2 = {}, {}, {}, {}
                    wo_wait = {}
                    for dh in range(2):
                        @blk.gpsimd
                        def _(g, dh=dh):
                            if wo_prev_pe[0] is not None:
                                g.wait_ge(pe.sem, wo_prev_pe[0])
                            g.wait_ge(dTs[dh].sem, dTs[dh].v)
                            for c2 in range(NB):
                                dTs[dh].inc(g.dma_start(
                                    woh_sb[:, c2, :],
                                    wfd["wo"][c2][:, 512 * dh:512 * (dh + 1)]), 16)
                            wo_wait[dh] = dTs[dh].v

                        for d2 in range(4 * dh, 4 * dh + 4):
                            @blk.tensor
                            def _(tn, d2=d2, dh=dh, W=W, atE=atE, xt_rd=xt_rd):
                                if d2 == 0:
                                    tn.wait_ge(at.sem, atE)
                                    tn.wait_ge(dv.sem, xt_rd)
                                if d2 == 4 * dh:
                                    tn.wait_ge(dTs[dh].sem, wo_wait[dh])
                                if d2 >= 4:
                                    tn.wait_ge(dv.sem, ypv[d2 - 4])
                                for c2 in range(NB):
                                    tn.matmul(pp_ps[d2 % 4][:, 0:W],
                                              woh_sb[:, c2, 128 * (d2 - 4 * dh):128 * (d2 - 4 * dh + 1)],
                                              att_sb[:, c2, 0:W], start=(c2 == 0), stop=False)
                                mm = tn.matmul(pp_ps[d2 % 4][:, 0:W],
                                               boe[0:1, 128 * d2:128 * (d2 + 1)],
                                               ones_r[0:1, 0:W], start=False, stop=True)
                                pe.inc(mm)
                                ppv[d2] = pe.v

                            @blk.vector
                            def _(v, d2=d2, W=W, qo=qo):
                                v.wait_ge(pe.sem, ppv[d2])
                                if d2 - 2 in stv2:
                                    v.wait_ge(pe.sem, stv2[d2 - 2])
                                ins = v.tensor_tensor(y_sb[:, d2, qo:qo + W],
                                                      pp_ps[d2 % 4][:, 0:W],
                                                      xt_sb[:, d2, 0:W], AX.add)
                                dv.inc(ins)
                                ypv[d2] = dv.v
                                v.wait_ge(dv.sem, dv.v)
                                ch(v, dv, v.tensor_tensor(sq_sb[:, d2 % 2, 0:W],
                                                          y_sb[:, d2, qo:qo + W],
                                                          y_sb[:, d2, qo:qo + W], AX.mult))
                                sqv2[d2] = dv.v

                            if d2 >= 1:
                                @blk.tensor
                                def _(tn, d2=d2, W=W, qo=qo):
                                    dd = d2 - 1
                                    tn.wait_ge(dv.sem, sqv2[dd])
                                    tn.matmul(sm_ps[0:1, 0:W], onesf_c[:],
                                              y_sb[:, dd, qo:qo + W],
                                              start=(dd == 0), stop=False)
                                    mm = tn.matmul(sq_ps[0:1, 0:W], onesf_c[:],
                                                   sq_sb[:, dd % 2, 0:W],
                                                   start=(dd == 0), stop=False)
                                    stv2[dd] = pe.inc(mm)
                        wo_prev_pe[0] = ppv[4 * dh + 3]

                    @blk.tensor
                    def _(tn, W=W, qo=qo):
                        dd = NB - 1
                        tn.wait_ge(dv.sem, sqv2[dd])
                        tn.matmul(sm_ps[0:1, 0:W], onesf_c[:], y_sb[:, dd, qo:qo + W],
                                  start=False, stop=True)
                        mm = tn.matmul(sq_ps[0:1, 0:W], onesf_c[:], sq_sb[:, dd % 2, 0:W],
                                       start=False, stop=True)
                        pe.inc(mm)
                    st_pe = pe.v

                    @blk.vector
                    def _(v, W=W, st_pe=st_pe):
                        v.wait_ge(pe.sem, st_pe)
                        ch(v, dv, v.tensor_scalar(rowA[0:1, 0:W], sm_ps[0:1, 0:W],
                                                  1.0 / D, None, AX.mult))
                        ch(v, dv, v.tensor_scalar(rowB[0:1, 0:W], sq_ps[0:1, 0:W],
                                                  1.0 / D, None, AX.mult))
                        ch(v, dv, v.tensor_tensor(rowD[0:1, 0:W], rowA[0:1, 0:W],
                                                  rowA[0:1, 0:W], AX.mult))
                        ch(v, dv, v.tensor_sub(rowB[0:1, 0:W], rowB[0:1, 0:W],
                                               rowD[0:1, 0:W]))
                    r1_dv = dv.v

                    if qi + 1 < len(WL):
                        emit_xt_prep(qi + 1, ypv[NB - 1], pp_ps)

                    @blk.scalar
                    def _(s, W=W, r1_dv=r1_dv):
                        s.wait_ge(dv.sem, r1_dv)
                        ch(s, ac, s.activation(rowB[0:1, 0:W], rowB[0:1, 0:W],
                                               AF.Ln, bias=epsB[0:1, :]))
                        ch(s, ac, s.activation(rowB[0:1, 0:W], rowB[0:1, 0:W],
                                               AF.Exp, scale=-0.5))
                    r1_ac = ac.v

                    @blk.vector
                    def _(v, W=W, r1_ac=r1_ac):
                        v.wait_ge(ac.sem, r1_ac)
                        ch(v, dv, v.scalar_tensor_tensor(rowC[0:1, 0:W], rowA[0:1, 0:W],
                                                         -1.0, rowB[0:1, 0:W],
                                                         op0=AX.mult, op1=AX.mult))
                    b1_dv = dv.v

                    @blk.tensor
                    def _(tn, W=W, b1_dv=b1_dv):
                        tn.wait_ge(dv.sem, b1_dv)
                        tn.matmul(rb_ps[:, 0:W], onesf_r[:], rowB[0:1, 0:W],
                                  start=True, stop=True)
                        pe.inc(tn.matmul(nb_ps[:, 0:W], onesf_r[:], rowC[0:1, 0:W],
                                         start=True, stop=True))
                    b1_pe = pe.v

                    @blk.vector
                    def _(v, W=W, qo=qo, qi=qi, b1_pe=b1_pe):
                        v.wait_ge(pe.sem, b1_pe)
                        for d2 in range(NB):
                            ch(v, dv, v.tensor_tensor(y_sb[:, d2, qo:qo + W],
                                                      y_sb[:, d2, qo:qo + W],
                                                      rb_ps[:, 0:W], AX.mult))
                            ch(v, dv, v.tensor_tensor(y_sb[:, d2, qo:qo + W],
                                                      y_sb[:, d2, qo:qo + W],
                                                      nb_ps[:, 0:W], AX.add))
                        if qi == 0:
                            ch(v, dv, v.tensor_scalar(y_sb[:, :, 0:1], y_sb[:, :, 0:1],
                                                      msk_sb[:], None, AX.mult))
                    nq_dv = dv.v
                    prev_epi_done[0] = nq_dv
                    bar_dv = xt_ready[0] if qi + 1 < len(WL) else nq_dv

                    @blk.tensor
                    def _(tn, bar_dv=bar_dv):
                        tn.wait_ge(dv.sem, bar_dv)

            b_end_dv, b_end_pe, b_end_ac = dv.v, pe.v, ac.v

        with ExitStack() as csc:
            EC = csc.enter_context
            wch = [EC(nc.sbuf_tensor(f"wch{i}", [128, D], BF16)) for i in range(8)]
            xr_sb = EC(nc.sbuf_tensor("xr_sb", [128, NB, 512], BF16))
            xk_sb = EC(nc.sbuf_tensor("xk_sb", [128, NB, 512], BF16))
            th_sb = EC(nc.sbuf_tensor("th_sb", [128, NB, 512], BF16))
            ka_sb = EC(nc.sbuf_tensor("ka_sb", [128, NB, 512], BF16))
            tmp_sb = EC(nc.sbuf_tensor("tmp_sb", [128, 512], F32))
            yp2_sb = EC(nc.sbuf_tensor("yp2", [128, NB, 512], F32))
            dump_sb = EC(nc.sbuf_tensor("dump_c", [128, D], F32))
            stat_sb = EC(nc.sbuf_tensor("stat_c", [128, 24], F32))
            out_sb = [EC(nc.sbuf_tensor(f"osb{i}", [128, D], F32)) for i in range(2)]

            NTC = len(CW)
            sl_sem = [dTs[0], dTs[1], dKs[0], dKs[1]]
            rw_name, rw_rhs = [], []
            for _ci in range(NTC):
                rw_name += ["wr", "wkf", "wvf"]
                rw_rhs += [xr_sb, xk_sb, ka_sb]

            with ExitStack() as psc:
                pr_ps = psc.enter_context(nc.psum_tensor("pr_ps", [128, NB, 512], F32))

                wv_ = {}
                slot_gate = {}
                mixv = {}
                halffree = {-2: (dv.sem, b_end_dv), -1: (dv.sem, b_end_dv)}
                reluA_dv, reluB_dv, sttA_dv, sttB_dv = {}, {}, {}, {}
                rAv, rBv, kAv, kBv, vAv, vBv = {}, {}, {}, {}, {}, {}
                trp = {}
                plan_id, plan_sq, plan_ln, plan_exp = {}, {}, {}, {}
                tanhA_p, tanhB_p = {}, {}
                varv, tsv = {}, {}
                osb_dma = {}
                osb_pred = {}
                first_loads = [True]
                TAILSEQ = [("id", 0), ("sq", 0), ("id", 1), ("sq", 1),
                           ("ln", 0), ("exp", 0), ("id", 2), ("sq", 2),
                           ("ln", 1), ("exp", 1), ("id", 3), ("sq", 3),
                           ("ln", 2), ("exp", 2), ("ln", 3), ("exp", 3)]

                def plan_tail(p, acB=None):
                    if acB is None:
                        acB = ac.v
                    ntb = CW[p] // 128
                    seq = [x for x in TAILSEQ if x[1] < ntb]
                    for i, (kind, k) in enumerate(seq):
                        {"id": plan_id, "sq": plan_sq,
                         "ln": plan_ln, "exp": plan_exp}[kind][(p, k)] = acB + i + 1
                    for k in range(ntb):
                        if k >= 2:
                            osb_pred[(p, k)] = dVs[k % 2].v + 16
                        else:
                            osb_pred[(p, k)] = osb_dma.get(k % 2)
                    return acB + len(seq)

                def emit_loads(r):
                    gates = dict(slot_gate)

                    @blk.gpsimd
                    def _(g, r=r, gates=gates, first=first_loads[0]):
                        if first:
                            g.wait_ge(dv.sem, b_end_dv)
                            g.wait_ge(pe.sem, b_end_pe)
                        for c in range(NB):
                            ss = sl_sem[c % 4]
                            g.wait_ge(ss.sem, ss.v)
                            if c in gates:
                                g.wait_ge(pe.sem, gates[c])
                            ss.inc(g.dma_start(wch[c][:], wfd[rw_name[r]][c]), 16)
                            wv_[(r, c)] = ss.v
                    first_loads[0] = False

                def wave(r, ci, wb, gates):
                    W = CW[ci]
                    h = ci % 2

                    @blk.tensor
                    def _(tn, r=r, W=W, h=h, wb=wb, gates=list(gates)):
                        for sem_, val in gates:
                            tn.wait_ge(sem_, val)
                        rhs = rw_rhs[r]
                        for c in range(NB):
                            if wb == 0:
                                tn.wait_ge(sl_sem[c % 4].sem, wv_[(r, c)])
                            for j in range(4):
                                mm = tn.matmul(
                                    pr_ps[:, 4 * h + j, 0:W],
                                    wch[c][:, 128 * (4 * wb + j):128 * (4 * wb + j + 1)],
                                    rhs[:, c, 0:W],
                                    start=(c == 0), stop=(c == NB - 1))
                            if wb == 1:
                                slot_gate[c] = pe.inc(mm)
                        if wb == 0:
                            pe.inc(mm)

                def emit_mix(mi):
                    @blk.vector
                    def _(v, mi=mi):
                        if mi == 0:
                            v.wait_ge(dv.sem, b_end_dv)
                            v.wait_ge(pe.sem, b_end_pe)
                        else:
                            v.wait_ge(pe.sem, kBv[mi - 1])
                        co, W = COFF[mi], CW[mi]
                        for d2 in range(NB):
                            ch(v, dv, v.tensor_sub(tmp_sb[:, 0:W],
                                                   y_sb[:, d2, co:co + W],
                                                   y_sb[:, d2, co - 1:co - 1 + W]))
                            ch(v, dv, v.scalar_tensor_tensor(
                                xr_sb[:, d2, 0:W], tmp_sb[:, 0:W],
                                mur_sb[:, d2:d2 + 1],
                                y_sb[:, d2, co - 1:co - 1 + W],
                                op0=AX.mult, op1=AX.add))
                            ch(v, dv, v.scalar_tensor_tensor(
                                xk_sb[:, d2, 0:W], tmp_sb[:, 0:W],
                                muk_sb[:, d2:d2 + 1],
                                y_sb[:, d2, co - 1:co - 1 + W],
                                op0=AX.mult, op1=AX.add))
                    mixv[mi] = dv.v

                def emit_relu(ci, wb):
                    @blk.vector
                    def _(v, ci=ci, wb=wb):
                        W, h = CW[ci], ci % 2
                        v.wait_ge(pe.sem, kAv[ci] if wb == 0 else kBv[ci])
                        for j in range(4):
                            ja = 4 * wb + j
                            ch(v, dv, v.tensor_scalar(tmp_sb[:, 0:W],
                                                      pr_ps[:, 4 * h + j, 0:W],
                                                      0.0, None, AX.max))
                            ch(v, dv, v.tensor_tensor(ka_sb[:, ja, 0:W],
                                                      pr_ps[:, 4 * h + j, 0:W],
                                                      tmp_sb[:, 0:W], AX.mult))
                    (reluA_dv if wb == 0 else reluB_dv)[ci] = dv.v

                def emit_stt(ci, wb):
                    @blk.vector
                    def _(v, ci=ci, wb=wb):
                        W, h, co = CW[ci], ci % 2, COFF[ci]
                        v.wait_ge(pe.sem, vAv[ci] if wb == 0 else vBv[ci])
                        v.wait_ge(ac.sem, tanhA_p[ci] if wb == 0 else tanhB_p[ci])
                        if wb == 0 and ci >= 1:
                            v.wait_ge(pe.sem, trp[ci - 1][CW[ci - 1] // 128 - 1])
                        for j in range(4):
                            ja = 4 * wb + j
                            ch(v, dv, v.scalar_tensor_tensor(
                                tmp_sb[:, 0:W], th_sb[:, ja, 0:W], 1.0,
                                pr_ps[:, 4 * h + j, 0:W],
                                op0=AX.add, op1=AX.mult))
                            ch(v, dv, v.tensor_tensor(yp2_sb[:, ja, 0:W],
                                                      tmp_sb[:, 0:W],
                                                      y_sb[:, ja, co:co + W], AX.add))
                    (sttA_dv if wb == 0 else sttB_dv)[ci] = dv.v

                def emit_tr(p, tbk):
                    @blk.tensor
                    def _(tn, p=p, tbk=tbk):
                        hp = p % 2
                        tn.wait_ge(dv.sem, sttB_dv[p])
                        if tbk >= 2:
                            tn.wait_ge(ac.sem, plan_sq[(p, tbk - 2)])
                        base = 4 * hp + 2 * (tbk % 2)
                        dstv = pr_ps[:, base:base + 2, :].rearrange("p a w -> p (a w)")
                        for b2 in range(NB):
                            mm = tn.transpose(dstv[:, 128 * b2:128 * (b2 + 1)],
                                              yp2_sb[:, b2, 128 * tbk:128 * (tbk + 1)],
                                              idf[:])
                        pe.inc(mm)
                    trp.setdefault(p, {})[tbk] = pe.v

                def emit_varchain(p, tbk):
                    @blk.vector
                    def _(v, p=p, tbk=tbk):
                        c0 = 6 * tbk
                        v.wait_ge(ac.sem, plan_sq[(p, tbk)])
                        ch(v, dv, v.tensor_scalar(stat_sb[:, c0 + 2:c0 + 3],
                                                  stat_sb[:, c0 + 0:c0 + 1],
                                                  1.0 / D, None, AX.mult))
                        ch(v, dv, v.tensor_scalar(stat_sb[:, c0 + 3:c0 + 4],
                                                  stat_sb[:, c0 + 1:c0 + 2],
                                                  1.0 / D, None, AX.mult))
                        ch(v, dv, v.tensor_tensor(stat_sb[:, c0 + 5:c0 + 6],
                                                  stat_sb[:, c0 + 2:c0 + 3],
                                                  stat_sb[:, c0 + 2:c0 + 3], AX.mult))
                        ch(v, dv, v.tensor_sub(stat_sb[:, c0 + 3:c0 + 4],
                                               stat_sb[:, c0 + 3:c0 + 4],
                                               stat_sb[:, c0 + 5:c0 + 6]))
                    varv[(p, tbk)] = dv.v

                def emit_nmr_ts(p, tbk):
                    @blk.vector
                    def _(v, p=p, tbk=tbk):
                        c0 = 6 * tbk
                        v.wait_ge(ac.sem, plan_exp[(p, tbk)])
                        ch(v, dv, v.scalar_tensor_tensor(stat_sb[:, c0 + 5:c0 + 6],
                                                         stat_sb[:, c0 + 2:c0 + 3],
                                                         -1.0,
                                                         stat_sb[:, c0 + 4:c0 + 5],
                                                         op0=AX.mult, op1=AX.mult))
                        ch(v, dv, v.tensor_scalar(out_sb[tbk % 2][:],
                                                  out_sb[tbk % 2][:],
                                                  stat_sb[:, c0 + 4:c0 + 5],
                                                  stat_sb[:, c0 + 5:c0 + 6],
                                                  AX.mult, AX.add))
                    tsv[(p, tbk)] = dv.v

                def emit_tail_act(p, ci, planned_end):
                    ntb = CW[p] // 128 if p is not None else 0
                    seq = [x for x in TAILSEQ if x[1] < ntb] if p is not None else []

                    @blk.scalar
                    def _(s, p=p, ci=ci, seq=list(seq)):
                        hp = p % 2 if p is not None else 0
                        if ci is not None:
                            W, h = CW[ci], ci % 2
                            s.wait_ge(pe.sem, rAv[ci])
                            if ci >= 1:
                                s.wait_ge(dv.sem, sttA_dv[ci - 1])
                            for j in range(4):
                                ins = s.activation(th_sb[:, j, 0:W],
                                                   pr_ps[:, 4 * h + j, 0:W],
                                                   AF.Tanh, bias=zb[:], scale=0.5)
                            ac.inc(ins)
                            s.wait_ge(pe.sem, rBv[ci])
                            if ci >= 1:
                                s.wait_ge(dv.sem, sttB_dv[ci - 1])
                            for j in range(4):
                                ins = s.activation(th_sb[:, 4 + j, 0:W],
                                                   pr_ps[:, 4 * h + j, 0:W],
                                                   AF.Tanh, bias=zb[:], scale=0.5)
                            ac.inc(ins)
                        for kind, k in seq:
                            c0 = 6 * k
                            base = 4 * hp + 2 * (k % 2)
                            dstv = pr_ps[:, base:base + 2, :].rearrange(
                                "p a w -> p (a w)")
                            if kind == "id":
                                s.wait_ge(pe.sem, trp[p][k])
                                if osb_pred[(p, k)] is not None:
                                    s.wait_ge(dVs[k % 2].sem, osb_pred[(p, k)])
                                ch(s, ac, s.activation(out_sb[k % 2][:], dstv,
                                                       AF.Identity,
                                                       accum_out=stat_sb[:, c0:c0 + 1]))
                            elif kind == "sq":
                                ch(s, ac, s.activation(dump_sb[:], dstv, AF.Square,
                                                       accum_out=stat_sb[:, c0 + 1:c0 + 2]))
                            elif kind == "ln":
                                s.wait_ge(dv.sem, varv[(p, k)])
                                ch(s, ac, s.activation(stat_sb[:, c0 + 4:c0 + 5],
                                                       stat_sb[:, c0 + 3:c0 + 4],
                                                       AF.Ln, bias=epsB[:]))
                            else:
                                ch(s, ac, s.activation(stat_sb[:, c0 + 4:c0 + 5],
                                                       stat_sb[:, c0 + 4:c0 + 5],
                                                       AF.Exp, scale=-0.5))
                    assert ac.v == planned_end, (ac.v, planned_end)

                def emit_store(p, tbk, fo):
                    pred = osb_pred.get((p, tbk + 2))
                    if pred is not None:
                        assert pred == dVs[tbk % 2].v + 16, (pred, dVs[tbk % 2].v)

                    @blk.gpsimd
                    def _(g, p=p, tbk=tbk, fo=fo):
                        co = COFF[p]
                        g.wait_ge(dv.sem, tsv[(p, tbk)])
                        if fo:
                            g.wait_ge(dVs[tbk % 2].sem, dVs[tbk % 2].v)
                        dVs[tbk % 2].inc(g.dma_start(
                            out_d[co - 1 + 128 * tbk:co - 1 + 128 * (tbk + 1), :],
                            out_sb[tbk % 2][:]), 16)
                        osb_dma[tbk % 2] = dVs[tbk % 2].v

                emit_mix(0)
                emit_loads(0)
                for ci in range(NTC):
                    prev = ci - 1 if ci >= 1 else None
                    ac0 = ac.v
                    tanhA_p[ci] = ac0 + 1
                    tanhB_p[ci] = ac0 + 2
                    if prev is not None:
                        pend = plan_tail(prev, acB=ac0 + 2)
                    else:
                        pend = ac0 + 2
                    if prev is not None:
                        for tb in range(min(2, CW[prev] // 128)):
                            emit_tr(prev, tb)
                    wave(3 * ci, ci, 0, [(dv.sem, mixv[ci]), halffree[ci - 2]])
                    rAv[ci] = pe.v
                    wave(3 * ci, ci, 1, [(ac.sem, tanhA_p[ci])])
                    rBv[ci] = pe.v
                    if prev is not None:
                        for tb in range(2, CW[prev] // 128):
                            emit_tr(prev, tb)
                    emit_loads(3 * ci + 1)
                    if prev is not None:
                        for tb in range(CW[prev] // 128):
                            emit_varchain(prev, tb)
                            emit_nmr_ts(prev, tb)
                            emit_store(prev, tb, fo=(prev == 0 and tb < 2))
                        halffree[prev] = (ac.sem,
                                          plan_sq[(prev, CW[prev] // 128 - 1)])
                    wave(3 * ci + 1, ci, 0, [(ac.sem, tanhB_p[ci])])
                    kAv[ci] = pe.v
                    emit_relu(ci, 0)
                    wave(3 * ci + 1, ci, 1, [(dv.sem, reluA_dv[ci])])
                    kBv[ci] = pe.v
                    emit_loads(3 * ci + 2)
                    if ci + 1 < NTC:
                        emit_mix(ci + 1)
                    emit_relu(ci, 1)
                    wave(3 * ci + 2, ci, 0, [(dv.sem, reluB_dv[ci])])
                    vAv[ci] = pe.v
                    emit_stt(ci, 0)
                    wave(3 * ci + 2, ci, 1, [(dv.sem, sttA_dv[ci])])
                    vBv[ci] = pe.v
                    emit_stt(ci, 1)
                    emit_tail_act(prev, ci, pend)
                    if ci + 1 < NTC:
                        emit_loads(3 * (ci + 1))

                p = NTC - 1
                pend = plan_tail(p)
                for tb in range(CW[p] // 128):
                    emit_tr(p, tb)
                for tb in range(CW[p] // 128):
                    emit_varchain(p, tb)
                    emit_nmr_ts(p, tb)
                emit_tail_act(p, None, pend)
                for tb in range(CW[p] // 128):
                    emit_store(p, tb, fo=(p == 0 and tb < 2))

        @blk.gpsimd
        def _(g):
            g.wait_ge(dVs[0].sem, dVs[0].v)
            g.wait_ge(dVs[1].sem, dVs[1].v)

    return nc


_CACHE = {}


def kernel(L0_attn_sequence, L0_attn_ln_gamma, L0_attn_ln_beta,
           L0_attn_Wq, L0_attn_bq, L0_attn_Wk, L0_attn_bk,
           L0_attn_Wv, L0_attn_bv, L0_attn_Wo, L0_attn_bo, L0_attn_mu,
           L0_ffn_W_r, L0_ffn_W_k, L0_ffn_W_v, L0_ffn_mu_r, L0_ffn_mu_k):
    x = np.ascontiguousarray(np.asarray(L0_attn_sequence, dtype=np.float32))
    B, S, _ = x.shape
    TOKQ = S // 4
    NCORE = 8

    key = (S, TOKQ)
    if key not in _CACHE:
        _CACHE[key] = build(S, TOKQ)
    nc = _CACHE[key]

    def row(a):
        return np.ascontiguousarray(np.asarray(a, dtype=np.float32)).reshape(1, D)

    def col8(a):
        return np.ascontiguousarray(np.asarray(a, dtype=np.float32).reshape(NB, 128).T)

    wT = {n: np.ascontiguousarray(np.asarray(w, dtype=np.float32).T) for n, w in
          (("wq", L0_attn_Wq), ("wk", L0_attn_Wk), ("wv", L0_attn_Wv),
           ("wo", L0_attn_Wo), ("wr", L0_ffn_W_r), ("wkf", L0_ffn_W_k),
           ("wvf", L0_ffn_W_v))}
    shared = dict(wT)
    shared.update({"bq": row(L0_attn_bq), "bk": row(L0_attn_bk),
                   "bv": row(L0_attn_bv), "bo": row(L0_attn_bo),
                   "gam": col8(L0_attn_ln_gamma), "bet": col8(L0_attn_ln_beta),
                   "mur": col8(L0_ffn_mu_r), "muk": col8(L0_ffn_mu_k),
                   "mu": np.full((128, 1), float(np.asarray(L0_attn_mu).reshape(())),
                                 dtype=np.float32)})

    in_maps = []
    for c in range(NCORE):
        b = c // (NCORE // B)
        t0 = TOKQ * (c % (NCORE // B))
        if t0 == 0:
            xkv = x[b]
            xb = x[b, 0:1]
            msk = np.zeros((128, 1), dtype=np.float32)
        else:
            xkv = np.ascontiguousarray(
                np.concatenate([x[b, t0:], x[b, :t0]], axis=0))
            xb = np.ascontiguousarray(x[b, t0 - 1:t0])
            msk = np.ones((128, 1), dtype=np.float32)
        m = dict(shared)
        m.update({"xkv": xkv, "xb": xb, "msk": msk})
        in_maps.append(m)

    res = run_bass_kernel_spmd(nc, in_maps, core_ids=list(range(NCORE)))
    out = np.empty((B, S, D), dtype=np.float32)
    for c in range(NCORE):
        b = c // (NCORE // B)
        t0 = TOKQ * (c % (NCORE // B))
        out[b, t0:t0 + TOKQ] = res.results[c]["out"]
    return out
